# version 11 (final)
# baseline (speedup 1.0000x reference)
import numpy as np
import concourse.bass as bass
import concourse.mybir as mybir
from concourse.bass_utils import run_bass_kernel_spmd

F32 = mybir.dt.float32
BF16 = mybir.dt.bfloat16
I32 = mybir.dt.int32
AF = mybir.ActivationFunctionType
ALU = mybir.AluOpType
AX = mybir.AxisListType

NCORES = 8
D = 2048
CH = 16
TILE = 256
S = TILE // 128
TPC = 4096
NT_FULL = TPC // TILE
FT = CH * TILE
ATT_SCALE = 128.0 ** -0.5
EPS_LN = 1e-5
EPS_DIV = 1e-9
MAGIC8 = 0x5F3759DF >> 8


class MK:
    def __init__(self, nt=NT_FULL, debug_outputs=False, use_cc=True, stop_after=99):
        self.nt = nt
        self.dbg = debug_outputs
        self.use_cc = use_cc
        self.stop_after = stop_after
        self.nc = bass.Bass("TRN2", target_bir_lowering=False)
        from collections import defaultdict
        self.dcnt = defaultdict(int)

    def wload_ocgroup(self, eng, sb, dram, g, ncol=D):
        out = bass.AP(sb, g * 512, [[sb.shape[1], 128], [ncol, CH], [1, 512]])
        in_ = dram[:, g * 512:(g + 1) * 512].rearrange("(k p) o -> p k o", p=128)
        eng.dma_start(out=out, in_=in_).then_inc(self.sem["s_wt"], 16)

    def dma(self, eng, semname, out, in_):
        sem = self.sem[semname]
        c = self.dcnt[semname]
        if c:
            eng.wait_ge(sem, 16 * c)
        ins = eng.dma_start(out=out, in_=in_)
        ins.then_inc(sem, 16)
        self.dcnt[semname] = c + 1
        return ins


    def declare(self):
        nc = self.nc
        nt = self.nt
        ntok = nt * TILE
        self.x_in = nc.dram_tensor("x", [ntok, D], F32, kind="ExternalInput")
        self.x8_in = nc.dram_tensor("x8", [8, D], F32, kind="ExternalInput")
        self.wqt_in = nc.dram_tensor("wqt", [D, D], F32, kind="ExternalInput")
        self.wkvt_in = nc.dram_tensor("wkvt", [D, 2 * D], F32, kind="ExternalInput")
        self.wot_in = nc.dram_tensor("wot", [D, D], F32, kind="ExternalInput")
        self.w1t_in = nc.dram_tensor("w1t", [D, D], F32, kind="ExternalInput")
        self.w3t_in = nc.dram_tensor("w3t", [D, D], F32, kind="ExternalInput")
        self.w2t_in = nc.dram_tensor("w2t", [D, D], F32, kind="ExternalInput")
        self.wrt_in = nc.dram_tensor("wrt", [128, CH * 8], F32, kind="ExternalInput")
        self.ident_in = nc.dram_tensor("ident", [128, 128], F32, kind="ExternalInput")
        self.masks_in = nc.dram_tensor("masks", [128, 4], F32, kind="ExternalInput")
        self.pmsel_in = nc.dram_tensor("pmsel", [128, 8], F32, kind="ExternalInput")
        self.out = nc.dram_tensor("out", [ntok, D], F32, kind="ExternalOutput")

        sk = "ExternalOutput" if self.dbg else "Internal"
        self.y_sp = nc.dram_tensor("y_sp", [nt, 128, FT], BF16, kind=sk)
        self.q_sp = nc.dram_tensor("q_sp", [nt, 128, FT], BF16, kind=sk)
        self.kvc_sp = nc.dram_tensor("kvc_sp", [nt, 128, FT], F32, kind=sk)
        self.kc_sp = nc.dram_tensor("kc_sp", [nt, 128, FT], F32, kind=sk)
        self.y3_sp = nc.dram_tensor("y3_sp", [nt, 128, FT], BF16, kind=sk)
        if self.dbg:
            self.s2_dbg = nc.dram_tensor("s2_dbg", [128, 128], F32, kind="ExternalOutput")
            self.tots_dbg = nc.dram_tensor("tots_dbg", [128 * NCORES, 64], F32, kind="ExternalOutput")
        self.out8m_sp = nc.dram_tensor("out8m_sp", [8, D], F32, kind="Internal")
        self.x28_sp = nc.dram_tensor("x28_sp", [8, D], F32, kind="Internal")
        self.s2_ci = nc.dram_tensor("s2_ci", [128, 128], F32, kind="Internal")
        self.s2_co = nc.dram_tensor("s2_co", [128, 128], F32, kind="Internal")
        self.tots_ci = nc.dram_tensor("tots_ci", [128, 64], F32, kind="Internal")
        self.tots_co = nc.dram_tensor("tots_co", [128 * NCORES, 64], F32, kind="Internal")
        self.s2b_ci = nc.dram_tensor("s2b_ci", [128, 128], F32, kind="Internal")
        self.s2b_co = nc.dram_tensor("s2b_co", [128, 128], F32, kind="Internal")
        self.wkv_bf = nc.dram_tensor("wkv_bf", [D, 2 * D], BF16, kind="Internal")
        self.wq_bf = nc.dram_tensor("wq_bf", [D, D], BF16, kind="Internal")
        self.wo_bf = nc.dram_tensor("wo_bf", [D, D], BF16, kind="Internal")
        self.w1_bf = nc.dram_tensor("w1_bf", [D, D], BF16, kind="Internal")
        self.w3_bf = nc.dram_tensor("w3_bf", [D, D], BF16, kind="Internal")
        self.w2_bf = nc.dram_tensor("w2_bf", [D, D], BF16, kind="Internal")

    def alloc_sbuf(self):
        nc = self.nc
        A = nc.alloc_sbuf_tensor_at
        BASE = 16512
        RW1 = BASE
        RW2 = BASE + 65536
        B = BASE + 131072
        Pb = B + 73728

        self.w1_sb = A("w1_sb", [128, CH * D], BF16, offset=RW1)
        self.w3_sb = A("w3_sb", [128, CH * D], BF16, offset=RW2)
        self.wkv_sb = A("wkv_sb", [128, CH * 2 * D], BF16, offset=RW1)
        self.wq_sb = A("wq_sb", [128, CH * D], BF16, offset=RW1)
        self.w2s_sb = A("w2s_sb", [128, CH * D], BF16, offset=RW2)
        self.wo_sb = A("wo_sb", [128, CH * D], BF16, offset=RW1)

        self.xsl = A("xsl", [128, 3 * D], F32, offset=B)
        self.ytk = A("ytk", [128, 2 * D], BF16, offset=B + 24576)
        self.ycm = A("ycm", [128, 2 * FT], BF16, offset=B + 32768)
        self.h1 = A("h1", [128, FT], BF16, offset=B + 49152)
        self.h2t = A("h2t", [128, 2 * D], BF16, offset=B + 57344)
        self.scr = A("scr", [128, D], F32, offset=B + 65536)

        self.ycm2 = A("ycm2", [128, 2 * FT], BF16, offset=B)
        self.kch = A("kch", [128, 3 * TILE], F32, offset=B + 16384)
        self.vch = A("vch", [128, 3 * TILE], F32, offset=B + 19456)
        self.kvch = A("kvch", [128, 3 * TILE], F32, offset=B + 22528)
        self.tmin = A("tmin", [128, 3 * TILE], F32, offset=B + 25600)
        self.texp = A("texp", [128, 3 * TILE], F32, offset=B + 28672)
        self.kvcst = A("kvcst", [128, FT], F32, offset=B + 31744)
        self.kcst = A("kcst", [128, FT], F32, offset=B + 48128)
        self.w2f = A("w2f", [128, 4 * D], BF16, offset=B)

        FB = B + 16384
        self.fs2 = A("fs2", [128, 128], F32, offset=FB)
        self.fx8 = A("fx8", [8, D], F32, offset=FB + 512)
        self.fo8 = A("fo8", [8, D], F32, offset=FB + 8704)
        self.fx18 = A("fx18", [8, D], F32, offset=FB + 16896)
        self.fy8 = A("fy8", [8, D], BF16, offset=FB + 25088)

        self.ycm3 = A("ycm3", [128, 2 * FT], BF16, offset=B)
        self.qst = A("qst", [128, FT], BF16, offset=B + 16384)
        self.tmin3 = A("tmin3", [128, 2 * TILE], F32, offset=B + 24576)
        self.texp3 = A("texp3", [128, 2 * TILE], F32, offset=B + 26624)

        C = RW2
        self.qb = A("qb", [128, FT], BF16, offset=C)
        self.kvcb = A("kvcb", [128, FT], F32, offset=C + 8192)
        self.kcb = A("kcb", [128, FT], F32, offset=C + 24576)
        self.ratio = A("ratio", [128, 2 * FT], BF16, offset=C + 40960)
        self.xs3 = A("xs3", [128, 3 * D], F32, offset=C + 57344)
        self.x2b = A("x2b", [128, S * D], F32, offset=C + 81920)
        self.ytk3 = A("ytk3", [128, 2 * D], BF16, offset=C + 98304)
        self.y3cm = A("y3cm", [128, FT], BF16, offset=C + 106496)
        self.numb = A("numb", [128, 8 * TILE], F32, offset=C + 114688)
        self.denb = A("denb", [128, 8 * TILE], F32, offset=C + 122880)
        self.lnb = A("lnb", [128, 8 * TILE], F32, offset=C + 131072)
        self.recb = self.denb

        self.ycm4 = A("ycm4", [128, 2 * FT], BF16, offset=B)
        self.h14 = A("h14", [128, FT], BF16, offset=B + 16384)
        self.h2t4 = A("h2t4", [128, 2 * D], BF16, offset=B + 24576)

        self.fs2b = A("fs2b", [128, 128], F32, offset=B + 40960)
        self.fo8b = A("fo8b", [8, D], F32, offset=B + 41472)
        self.fx28 = A("fx28", [8, D], F32, offset=B + 49664)
        self.w2s5 = A("w2s5", [128, 4 * D], BF16, offset=B + 16384)

        p = Pb
        def per(name, shape, dtype):
            nonlocal p
            t = A(name, shape, dtype, offset=p)
            sz = int(np.prod(shape[1:])) * mybir.dt.size(dtype)
            p += (sz + 31) // 32 * 32
            return t
        self.consts = per("consts", [128, 8], F32)
        self.magic = per("magic", [128, 1], I32)
        self.ident_bf = per("ident_bf", [128, 128], BF16)
        self.wrt_sb = per("wrt_sb", [128, CH * 8], BF16)
        self.masks_sb = per("masks_sb", [128, 4], F32)
        self.pmsel_sb = per("pmsel_sb", [128, 8], F32)
        self.ckv = per("ckv", [128, 16], F32)
        self.ck = per("ck", [128, 16], F32)
        self.st_sum = per("st_sum", [128, 4], F32)
        self.st_sq = per("st_sq", [128, 4], F32)
        self.st_mean = per("st_mean", [128, 2], F32)
        self.st_t1 = per("st_t1", [128, 2], F32)
        self.st_t2 = per("st_t2", [128, 2], F32)
        self.st_v = per("st_v", [128, 2], F32)
        self.st_i1 = per("st_i1", [128, 2], I32)
        self.st_i2 = per("st_i2", [128, 2], I32)
        self.st_r = per("st_r", [128, 4], F32)
        self.st_rs = per("st_rs", [128, 2], F32)
        self.lg = per("lg", [128, 16], F32)
        self.rt1 = per("rt1", [128, 16], F32)
        self.rt2 = per("rt2", [128, 16], F32)
        self.rm1 = per("rm1", [128, 2], F32)
        self.rm2 = per("rm2", [128, 2], F32)
        self.rd = per("rd", [128, 2], F32)
        self.rg1 = per("rg1", [128, 2], F32)
        self.rg2 = per("rg2", [128, 2], F32)
        self.rsl = per("rsl", [128, 2], F32)
        self.G = per("G", [128, 16], BF16)
        self.s2sb = per("s2sb", [128, 128], F32)
        self.totsb = per("totsb", [128, 64], F32)
        self.n8kvc = per("n8kvc", [128, 128], F32)
        self.n8kc = per("n8kc", [128, 128], F32)
        self.dkv = per("dkv", [128, 16], F32)
        self.dk = per("dk", [128, 16], F32)
        self.pkv = per("pkv", [128, 16], F32)
        self.pk = per("pk", [128, 16], F32)
        self.adjkvc = per("adjkvc", [128, 128], F32)
        self.adjkc = per("adjkc", [128, 128], F32)
        self.q8m = per("q8m", [128, 128], BF16)
        self.y8c = per("y8c", [128, 128], BF16)
        self.fs2c = per("fs2c", [128, 128], BF16)
        self.fk8 = per("fk8", [128, 128], F32)
        self.fv8 = per("fv8", [128, 128], F32)
        self.fkv8 = per("fkv8", [128, 128], F32)
        self.gth = per("gth", [128, 64], F32)
        self.accA = per("accA", [128, 64], F32)
        self.accB = per("accB", [128, 64], F32)
        assert p <= 229376, f"persistent overflow {p}"

        P = nc.alloc_psum_tensor
        self.pmA = P("pmA", [128, 512], F32)
        self.pmB = P("pmB", [128, 512], F32)
        self.ptkA = P("ptkA", [128, 512], F32)
        self.ptkB = P("ptkB", [128, 512], F32)
        self.ptr = P("ptr", [128, 2048], BF16)
        self.ps2 = P("ps2", [128, 512], F32)
        self.pmisc = P("pmisc", [128, 512], F32)

    def semaphores(self, ctx):
        names = [
            "s_x", "s_yw", "s_yr", "s_qw", "s_qr", "s_kw", "s_kr",
            "s_3w", "s_3r", "s_ow", "s_wt", "s_fx8", "s_cc",
            "s_ptp", "s_plg", "s_ph1", "s_ph2", "s_pkv", "s_pq", "s_pao",
            "s_pfx", "s_ps2", "s_es2",
            "s_etp", "s_elg", "s_eh1", "s_eh2", "s_ekv", "s_eq", "s_eao",
            "s_ast", "s_aex", "s_aln", "s_asl", "s_dmin", "s_dd", "s_dden",
            "s_dy", "s_dg", "s_dkv", "s_dr", "s_dx2", "s_dfx", "s_afx", "s_gkv",
            "s_fx", "s_fy", "s_fkv", "s_fq", "s_f3", "s_init", "s_x2",
            "s_wc", "s_wcw2", "s_wcq", "s_wco", "s_wc13", "s_wkv2",
        ]
        self.sem = {}
        for n in names:
            self.sem[n] = ctx.enter_context(self.nc.semaphore(n))


    def rsqrt_chain(self, vector, v_ap, out_ap, w):
        vector.tensor_scalar(self.st_i1[:, 0:w], v_ap.bitcast(I32), 9, None, ALU.logical_shift_right)
        vector.drain()
        vector.tensor_scalar(self.st_i2[:, 0:w], self.st_i1[:, 0:w], MAGIC8, -1, ALU.subtract, ALU.mult)
        vector.drain()
        vector.tensor_scalar(self.st_i1[:, 0:w], self.st_i2[:, 0:w], 8, None, ALU.logical_shift_left)
        vector.drain()
        cur = self.st_i1[:, 0:w].bitcast(F32)
        for it in range(4):
            nxt = out_ap if it == 3 else (self.st_r[:, 0:w] if it % 2 == 0 else self.st_r[:, 2:2 + w])
            vector.tensor_mul(self.st_t1[:, 0:w], cur, cur)
            vector.drain()
            vector.tensor_mul(self.st_t2[:, 0:w], self.st_t1[:, 0:w], v_ap)
            vector.drain()
            vector.tensor_scalar(self.st_t1[:, 0:w], self.st_t2[:, 0:w], -0.5, 1.5, ALU.mult, ALU.add)
            vector.drain()
            vector.tensor_mul(nxt, cur, self.st_t1[:, 0:w])
            vector.drain()
            cur = nxt

    def ln_finalize_apply(self, vector, x_slices, ytk_dst, sem_dy, sem_fx, b=0):
        sem = self.sem
        vector.tensor_scalar(self.st_mean[:, 0:S], self.st_sum[:, b:b + S], 1.0 / D, None, ALU.mult)
        vector.tensor_scalar(self.st_v[:, 0:S], self.st_sq[:, b:b + S], 1.0 / D, None, ALU.mult)
        vector.drain()
        vector.tensor_mul(self.st_t1[:, 0:S], self.st_mean[:, 0:S], self.st_mean[:, 0:S])
        vector.drain()
        vector.tensor_sub(self.st_v[:, 0:S], self.st_v[:, 0:S], self.st_t1[:, 0:S])
        vector.drain()
        vector.tensor_scalar(self.st_v[:, 0:S], self.st_v[:, 0:S], EPS_LN, None, ALU.add)
        vector.drain()
        self.rsqrt_chain(vector, self.st_v[:, 0:S], self.st_rs[:, 0:S], S)
        for s in range(S):
            vector.tensor_scalar(
                ytk_dst[s], x_slices[s],
                self.st_mean[:, s:s + 1], self.st_rs[:, s:s + 1],
                ALU.subtract, ALU.mult).then_inc(sem[sem_dy], 1)

    def router_dve(self, vector, i, s):
        sem = self.sem
        idx = i * S + s
        L = self.lg[:, s * 8:(s + 1) * 8]
        T1 = self.rt1[:, s * 8:(s + 1) * 8]
        T2 = self.rt2[:, s * 8:(s + 1) * 8]
        vector.wait_ge(sem["s_plg"], idx + 1)
        vector.tensor_copy(L, self.pmisc[:, s * 8:(s + 1) * 8]).then_inc(sem["s_elg"], 1)
        vector.drain()
        vector.reduce_max(self.rm1[:, s:s + 1], L, axis=AX.X)
        vector.drain()
        vector.tensor_scalar(T1, L, self.rm1[:, s:s + 1], None, ALU.is_ge)
        vector.drain()
        vector.scalar_tensor_tensor(T2, T1, -1e30, L, ALU.mult, ALU.add)
        vector.drain()
        vector.reduce_max(self.rm2[:, s:s + 1], T2, axis=AX.X)
        vector.drain()
        vector.tensor_sub(self.rd[:, s:s + 1], self.rm1[:, s:s + 1], self.rm2[:, s:s + 1])
        vector.drain()
        vector.tensor_scalar(self.rd[:, s:s + 1], self.rd[:, s:s + 1], 1e-30, None, ALU.add).then_inc(sem["s_dd"], 1)
        vector.tensor_scalar(T2, T2, self.rm2[:, s:s + 1], None, ALU.is_ge)
        vector.wait_ge(sem["s_asl"], idx + 1)
        vector.reciprocal(self.rg1[:, s:s + 1], self.rd[:, s:s + 1])
        vector.drain()
        vector.tensor_mul(self.rg1[:, s:s + 1], self.rg1[:, s:s + 1], self.rsl[:, s:s + 1])
        vector.drain()
        vector.tensor_scalar(self.rg2[:, s:s + 1], self.rg1[:, s:s + 1], -1.0, 1.0, ALU.mult, ALU.add)
        vector.drain()
        vector.tensor_scalar(T1, T1, self.rg1[:, s:s + 1], None, ALU.mult)
        vector.drain()
        vector.scalar_tensor_tensor(self.G[:, s * 8:(s + 1) * 8], T2, self.rg2[:, s:s + 1],
                                    T1, ALU.mult, ALU.add).then_inc(sem["s_dg"], 1)


    def p1_sync(self, sync):
        sem = self.sem

        def yspill(j):
            sync.wait_ge(sem["s_etp"], (j + 1) * S * 2)
            self.dma(sync, "s_yw", self.y_sp[j, :, :],
                     self.ycm[:, (j % 2) * FT:(j % 2 + 1) * FT])

        for i in range(self.nt):
            for s in range(S):
                idx = i * S + s
                if idx >= 3:
                    sync.wait_ge(sem["s_dy"], idx - 2)
                self.dma(sync, "s_x",
                         self.xsl[:, (idx % 3) * D:(idx % 3 + 1) * D],
                         self.x_in[idx * 128:(idx + 1) * 128, :])
            if i >= 1:
                yspill(i - 1)
        yspill(self.nt - 1)

    def p1_gpsimd(self, gpsimd):
        sem = self.sem
        gpsimd.memset(self.consts[:, 0:1], EPS_LN)
        gpsimd.memset(self.consts[:, 1:2], EPS_DIV)
        gpsimd.memset(self.consts[:, 2:3], -1.0)
        gpsimd.memset(self.magic[:, :], MAGIC8)
        gpsimd.memset(self.ckv[:, :], 0.0)
        gpsimd.memset(self.ck[:, :], 0.0).then_inc(sem["s_init"], 1)
        gpsimd.dma_start(out=self.ident_bf[:, :], in_=self.ident_in[:, :]).then_inc(sem["s_wt"], 16)
        gpsimd.dma_start(out=self.masks_sb[:, :], in_=self.masks_in[:, :]).then_inc(sem["s_wt"], 16)
        gpsimd.dma_start(out=self.pmsel_sb[:, :], in_=self.pmsel_in[:, :]).then_inc(sem["s_wt"], 16)
        gpsimd.dma_start(
            out=self.wrt_sb[:, :], in_=self.wrt_in[:, :],
        ).then_inc(sem["s_wt"], 16)
        for g in range(4):
            gpsimd.wait_ge(sem["s_wt"], 16 * (4 + g))
            self.wload_ocgroup(gpsimd, self.w1_sb, self.w1t_in, g)
        for g in range(4):
            gpsimd.wait_ge(sem["s_wt"], 16 * (8 + g))
            self.wload_ocgroup(gpsimd, self.w3_sb, self.w3t_in, g)
        self.dcnt["s_wt"] = 12
        gpsimd.dma_start(out=self.wkv_bf[:, :], in_=self.wkvt_in[:, :]).then_inc(sem["s_wc"], 16)
        gpsimd.dma_start(out=self.w2_bf[:, :], in_=self.w2t_in[:, :]).then_inc(sem["s_wcw2"], 16)
        gpsimd.dma_start(out=self.wq_bf[:, :], in_=self.wqt_in[:, :]).then_inc(sem["s_wcq"], 16)
        gpsimd.wait_ge(sem["s_ps2"], 2)
        self.dma(gpsimd, "s_fx8", self.s2_ci[:, :], self.s2sb[:, :])
        gpsimd.wait_ge(sem["s_fx8"], 16 * self.dcnt["s_fx8"])
        if self.use_cc:
            gpsimd.collective_compute(
                "AllReduce", ALU.add, replica_groups=[list(range(NCORES))],
                ins=[self.s2_ci.ap()], outs=[self.s2_co.ap()],
            ).then_inc(sem["s_cc"], 1)
        else:
            self.dma(gpsimd, "s_fx8", self.s2_co[:, :], self.s2_ci[:, :])
            gpsimd.wait_ge(sem["s_fx8"], 16 * self.dcnt["s_fx8"])
            gpsimd.nop().then_inc(sem["s_cc"], 1)

    def p1_act(self, scalar):
        sem = self.sem

        def stats(j):
            for s in range(S):
                idx = j * S + s
                scalar.wait_ge(sem["s_x"], 16 * (idx + 1))
                if idx:
                    scalar.wait_ge(sem["s_ast"], idx)
                xa = self.xsl[:, (idx % 3) * D:(idx % 3 + 1) * D]
                col = (j % 2) * S + s
                scalar.activation(self.scr[:, :], xa, AF.Copy, accum_out=self.st_sum[:, col:col + 1])
                scalar.drain()
                scalar.activation(self.scr[:, :], xa, AF.Square,
                                  accum_out=self.st_sq[:, col:col + 1]).then_inc(sem["s_ast"], 1)

        stats(0)
        for i in range(self.nt):
            for s in range(S):
                scalar.wait_ge(sem["s_dd"], i * S + s + 1)
                scalar.activation(self.rsl[:, s:s + 1], self.rd[:, s:s + 1], AF.Silu).then_inc(sem["s_asl"], 1)
            for occ in range(CH):
                k = i * CH + occ
                scalar.wait_ge(sem["s_ph1"], k + 1)
                bank = self.pmA if occ % 2 == 0 else self.pmB
                scalar.activation(self.h1[:, occ * TILE:(occ + 1) * TILE],
                                  bank[:, 0:TILE], AF.Silu).then_inc(sem["s_eh1"], 1)
            if i + 1 < self.nt:
                stats(i + 1)
            for s in range(S):
                idx = i * S + s
                if idx >= 2:
                    scalar.wait_ge(sem["s_es2"], idx - 1)
                for ob in range(4):
                    k = idx * 4 + ob
                    scalar.wait_ge(sem["s_ph2"], k + 1)
                    bank = self.ptkA if k % 2 == 0 else self.ptkB
                    scalar.activation(
                        self.h2t[:, (idx % 2) * D + ob * 512:(idx % 2) * D + (ob + 1) * 512],
                        bank[:, 0:512], AF.Silu).then_inc(sem["s_eh2"], 1)

    def p1_dve(self, vector):
        sem = self.sem
        for i in range(self.nt):
            vector.wait_ge(sem["s_ast"], i * S + S)
            xs = [self.xsl[:, ((i * S + s) % 3) * D:((i * S + s) % 3 + 1) * D] for s in range(S)]
            yd = [self.ytk[:, s * D:(s + 1) * D] for s in range(S)]
            self.ln_finalize_apply(vector, xs, yd, "s_dy", "s_fx", b=(i % 2) * S)
            for s in range(S):
                for g in range(2):
                    k = (i * S + s) * 2 + g
                    vector.wait_ge(sem["s_ptp"], k + 1)
                    for j in range(8):
                        c = g * 8 + j
                        ins = vector.tensor_copy(
                            self.ycm[:, (i % 2) * FT + c * TILE + s * 128:
                                     (i % 2) * FT + c * TILE + s * 128 + 128],
                            self.ptr[:, (g * 8 + j) * 128:(g * 8 + j + 1) * 128])
                    ins.then_inc(sem["s_etp"], 1)
            for s in range(S):
                self.router_dve(vector, i, s)
        vector.wait_ge(sem["s_ps2"], 1)
        vector.tensor_copy(self.s2sb[:, :], self.ps2[:, 0:128]).then_inc(sem["s_ps2"], 1)

    def p1_tensor(self, tensor):
        sem = self.sem
        nt = self.nt
        for i in range(nt):
            for s in range(S):
                idx = i * S + s
                tensor.wait_ge(sem["s_dy"], idx + 1)
                if i >= 2:
                    tensor.wait_ge(sem["s_yw"], 16 * (i - 1))
                for g in range(2):
                    k = idx * 2 + g
                    if k >= 2:
                        tensor.wait_ge(sem["s_etp"], k - 1)
                    for j in range(8):
                        ins = tensor.transpose(
                            self.ptr[:, (g * 8 + j) * 128:(g * 8 + j + 1) * 128],
                            self.ytk[:, s * D + (g * 8 + j) * 128:s * D + (g * 8 + j + 1) * 128],
                            self.ident_bf[:, :])
                    ins.then_inc(sem["s_ptp"], 1)
            for s in range(S):
                idx = i * S + s
                tensor.wait_ge(sem["s_etp"], (idx + 1) * 2)
                if idx >= 1:
                    tensor.wait_ge(sem["s_elg"], idx)
                for icc in range(CH):
                    ins = tensor.matmul(
                        self.pmisc[:, s * 8:(s + 1) * 8],
                        self.ycm[:, (i % 2) * FT + icc * TILE + s * 128:
                                 (i % 2) * FT + icc * TILE + s * 128 + 128],
                        self.wrt_sb[:, icc * 8:(icc + 1) * 8],
                        start=(icc == 0), stop=(icc == CH - 1))
                ins.then_inc(sem["s_plg"], 1)
            for occ in range(CH):
                k = i * CH + occ
                bank = self.pmA if occ % 2 == 0 else self.pmB
                if k >= 2:
                    tensor.wait_ge(sem["s_eh1"], k - 1)
                if i == 0:
                    tensor.wait_ge(sem["s_wt"], 16 * (5 + occ // 4))
                for icc in range(CH):
                    ins = tensor.matmul(
                        bank[:, 0:TILE],
                        self.w1_sb[:, icc * D + occ * 128:icc * D + (occ + 1) * 128],
                        self.ycm[:, (i % 2) * FT + icc * TILE:(i % 2) * FT + (icc + 1) * TILE],
                        start=(icc == 0), stop=(icc == CH - 1))
                ins.then_inc(sem["s_ph1"], 1)
            for s in range(S):
                idx = i * S + s
                tensor.wait_ge(sem["s_eh1"], (i + 1) * CH)
                for ob in range(4):
                    k = idx * 4 + ob
                    bank = self.ptkA if k % 2 == 0 else self.ptkB
                    if k >= 2:
                        tensor.wait_ge(sem["s_eh2"], k - 1)
                    if i == 0:
                        tensor.wait_ge(sem["s_wt"], 16 * (9 + ob))
                    for icc in range(CH):
                        ins = tensor.matmul(
                            bank[:, 0:512],
                            self.h1[:, icc * TILE + s * 128:icc * TILE + s * 128 + 128],
                            self.w3_sb[:, icc * D + ob * 512:icc * D + (ob + 1) * 512],
                            start=(icc == 0), stop=(icc == CH - 1))
                    ins.then_inc(sem["s_ph2"], 1)
                tensor.wait_ge(sem["s_eh2"], (idx + 1) * 4)
                tensor.wait_ge(sem["s_dg"], idx + 1)
                first = (i == 0 and s == 0)
                last = (i == nt - 1 and s == S - 1)
                for c in range(CH):
                    ins = tensor.matmul(
                        self.ps2[:, c * 8:(c + 1) * 8],
                        self.h2t[:, (idx % 2) * D + c * 128:(idx % 2) * D + (c + 1) * 128],
                        self.G[:, s * 8:(s + 1) * 8],
                        start=(first and c == 0), stop=(last and c == CH - 1))
                if last:
                    ins.then_inc(sem["s_ps2"], 1)
                else:
                    ins.then_inc(sem["s_es2"], 1)

    def p1_scalar_dma(self, scalar, i):
        sem = self.sem
        scalar.wait_ge(sem["s_etp"], (i + 1) * S * 2)
        self.dma(scalar, "s_yw", self.y_sp[i, :, :],
                 self.ycm[:, (i % 2) * FT:(i % 2 + 1) * FT])


    def b_ast(self):
        return self.nt * S

    def p2_yload(self, scalar, j):
        sem = self.sem
        if j >= 2:
            scalar.wait_ge(sem["s_pkv"], 32 * (j - 1))
        scalar.wait_ge(sem["s_yw"], 16 * (j + 1))
        self.dma(scalar, "s_yr", self.ycm2[:, (j % 2) * FT:(j % 2 + 1) * FT],
                 self.y_sp[j, :, :])

    def p2_sync(self, sync):
        sem = self.sem
        sync.wait_ge(sem["s_ps2"], 2)
        sync.wait_ge(sem["s_wc"], 16)
        for g in range(4, 8):
            out = bass.AP(self.wkv_sb, g * 512, [[self.wkv_sb.shape[1], 128], [2 * D, CH], [1, 512]])
            in_ = self.wkv_bf[:, g * 512:(g + 1) * 512].rearrange("(k p) o -> p k o", p=128)
            self.dma(sync, "s_wkv2", out, in_)
        for i in range(self.nt):
            sync.wait_ge(sem["s_dkv"], i + 1)
            self.dma(sync, "s_kw", self.kvc_sp[i, :, :], self.kvcst[:, :])
            self.dma(sync, "s_kw", self.kc_sp[i, :, :], self.kcst[:, :])
        sync.wait_ge(sem["s_pfx"], 37)
        sync.wait_ge(sem["s_wcq"], 16)
        for g in range(2):
            self.dma(sync, "s_wkv2",
                     bass.AP(self.wq_sb, g * 512, [[self.wq_sb.shape[1], 128], [D, CH], [1, 512]]),
                     self.wq_bf[:, g * 512:(g + 1) * 512].rearrange("(k p) o -> p k o", p=128))

    def p2_tensor(self, tensor):
        sem = self.sem
        tensor.wait_ge(sem["s_ps2"], 2)
        tensor.wait_ge(sem["s_eh2"], 8 * self.nt)
        for i in range(self.nt):
            tensor.wait_ge(sem["s_yr"], 16 * (i + 1))
            for c in range(CH):
                kbank = self.pmA if c % 2 == 0 else self.ptkA
                vbank = self.pmB if c % 2 == 0 else self.ptkB
                n = i * 32 + 2 * c
                if i == 0:
                    tensor.wait_ge(sem["s_wt"], 16 * (self.wt_wkv0 + c // 4 + 1))
                    tensor.wait_ge(sem["s_wkv2"], 16 * (c // 4 + 1))
                if n - 3 > 0:
                    tensor.wait_ge(sem["s_ekv"], n - 3)
                for icc in range(CH):
                    ins = tensor.matmul(
                        kbank[:, 0:TILE],
                        self.wkv_sb[:, icc * 2 * D + c * 128:icc * 2 * D + (c + 1) * 128],
                        self.ycm2[:, (i % 2) * FT + icc * TILE:(i % 2) * FT + (icc + 1) * TILE],
                        start=(icc == 0), stop=(icc == CH - 1))
                ins.then_inc(sem["s_pkv"], 1)
                if n - 2 > 0:
                    tensor.wait_ge(sem["s_ekv"], n - 2)
                for icc in range(CH):
                    ins = tensor.matmul(
                        vbank[:, 0:TILE],
                        self.wkv_sb[:, icc * 2 * D + D + c * 128:icc * 2 * D + D + (c + 1) * 128],
                        self.ycm2[:, (i % 2) * FT + icc * TILE:(i % 2) * FT + (icc + 1) * TILE],
                        start=(icc == 0), stop=(icc == CH - 1))
                ins.then_inc(sem["s_pkv"], 1)
        self.p2tail_tensor(tensor)

    def p2_act(self, scalar):
        sem = self.sem
        scalar.wait_ge(sem["s_ps2"], 2)
        scalar.wait_ge(sem["s_eh2"], 8 * self.nt)
        for j in range(min(2, self.nt)):
            self.p2_yload(scalar, j)
        for i in range(self.nt):
            for c in range(CH):
                scalar.wait_ge(sem["s_dmin"], i * CH + c + 1)
                scalar.activation(self.texp[:, (c % 3) * TILE:(c % 3 + 1) * TILE],
                                  self.tmin[:, (c % 3) * TILE:(c % 3 + 1) * TILE],
                                  AF.Exp).then_inc(sem["s_aex"], 1)
            if i + 2 < self.nt:
                self.p2_yload(scalar, i + 2)
        self.p2tail_act(scalar)

    def p2_gpsimd(self, gpsimd):
        sem = self.sem
        gpsimd.wait_ge(sem["s_wc"], 16 * 1)
        wkv0 = self.dcnt["s_wt"]
        for g in range(4):
            c = self.dcnt["s_wt"]
            gpsimd.wait_ge(sem["s_wt"], 16 * c)
            out = bass.AP(self.wkv_sb, g * 512, [[self.wkv_sb.shape[1], 128], [2 * D, CH], [1, 512]])
            in_ = self.wkv_bf[:, g * 512:(g + 1) * 512].rearrange("(k p) o -> p k o", p=128)
            gpsimd.dma_start(out=out, in_=in_).then_inc(sem["s_wt"], 16)
            self.dcnt["s_wt"] = c + 1
        self.wt_wkv0 = wkv0
        for i in range(self.nt):
            for c in range(CH):
                gpsimd.wait_ge(sem["s_ekv"], i * 32 + 2 * c + 2)
                gpsimd.tensor_tensor(
                    self.kvch[:, (c % 3) * TILE:(c % 3 + 1) * TILE],
                    self.kch[:, (c % 3) * TILE:(c % 3 + 1) * TILE],
                    self.vch[:, (c % 3) * TILE:(c % 3 + 1) * TILE],
                    ALU.mult).then_inc(sem["s_gkv"], 1)
        self.p2tail_gpsimd(gpsimd)

    def p2_dve(self, vector):
        sem = self.sem
        nt = self.nt
        vector.wait_ge(sem["s_yw"], 16 * nt)
        vector.wait_ge(sem["s_eh2"], 8 * nt)
        for i in range(nt):
            if i >= 1:
                vector.wait_ge(sem["s_kw"], 32 * i)
            for cc in range(CH + 1):
                if cc < CH:
                    c = cc
                    n = i * 32 + 2 * c
                    kbank = self.pmA if c % 2 == 0 else self.ptkA
                    vbank = self.pmB if c % 2 == 0 else self.ptkB
                    vector.wait_ge(sem["s_pkv"], n + 1)
                    vector.tensor_scalar_min(
                        self.tmin[:, (c % 3) * TILE:(c % 3 + 1) * TILE],
                        kbank[:, 0:TILE], 0.0).then_inc(sem["s_dmin"], 1)
                    vector.wait_ge(sem["s_aex"], i * CH + c + 1)
                    vector.scalar_tensor_tensor(
                        self.kch[:, (c % 3) * TILE:(c % 3 + 1) * TILE],
                        kbank[:, 0:TILE], 0.0,
                        self.texp[:, (c % 3) * TILE:(c % 3 + 1) * TILE],
                        ALU.max, ALU.add).then_inc(sem["s_ekv"], 1)
                    vector.wait_ge(sem["s_pkv"], n + 2)
                    vector.tensor_copy(
                        self.vch[:, (c % 3) * TILE:(c % 3 + 1) * TILE],
                        vbank[:, 0:TILE]).then_inc(sem["s_ekv"], 1)
                if cc >= 1:
                    c = cc - 1
                    vector.wait_ge(sem["s_gkv"], i * CH + c + 1)
                    vector.tensor_tensor_scan(
                        self.kvcst[:, c * TILE:(c + 1) * TILE],
                        self.kvch[:, (c % 3) * TILE:(c % 3 + 1) * TILE],
                        self.kvch[:, (c % 3) * TILE:(c % 3 + 1) * TILE],
                        self.ckv[:, c:c + 1], ALU.add, ALU.bypass)
                    vector.tensor_tensor_scan(
                        self.kcst[:, c * TILE:(c + 1) * TILE],
                        self.kch[:, (c % 3) * TILE:(c % 3 + 1) * TILE],
                        self.kch[:, (c % 3) * TILE:(c % 3 + 1) * TILE],
                        self.ck[:, c:c + 1], ALU.add, ALU.bypass)
            vector.drain()
            for c in range(CH):
                vector.tensor_copy(self.ckv[:, c:c + 1],
                                   self.kvcst[:, c * TILE + TILE - 1:c * TILE + TILE])
                ins = vector.tensor_copy(self.ck[:, c:c + 1],
                                         self.kcst[:, c * TILE + TILE - 1:c * TILE + TILE])
            if i == 0:
                vector.drain()
                vector.tensor_copy(self.totsb[:, 32:48],
                                   bass.AP(self.kcst, 7, [[FT, 128], [TILE, CH]]))
                ins = vector.tensor_copy(self.totsb[:, 48:64],
                                         bass.AP(self.kvcst, 7, [[FT, 128], [TILE, CH]]))
            ins.then_inc(sem["s_dkv"], 1)
        self.p2tail_dve(vector)


    def p2tail_tensor(self, tensor):
        sem = self.sem
        nt = self.nt
        banks = [self.pmA, self.pmB, self.ptkA, self.ptkB]
        tensor.wait_ge(sem["s_ekv"], 32 * nt)
        tensor.wait_ge(sem["s_gkv"], CH * nt + 1)
        for b in range(4):
            tensor.wait_ge(sem["s_wt"], 16 * (self.wt_w2f0 + b))
            for il in range(4):
                icc = b * 4 + il
                for ob in range(4):
                    ins = tensor.matmul(
                        banks[ob][0:8, 0:512],
                        self.fs2c[:, icc * 8:(icc + 1) * 8],
                        self.w2f[:, il * D + ob * 512:il * D + (ob + 1) * 512],
                        start=(icc == 0), stop=(icc == CH - 1))
            ins.then_inc(sem["s_pfx"], 1)
        tensor.wait_ge(sem["s_dfx"], 3)
        for c in range(CH):
            ins = tensor.transpose(
                self.ptr[:, c * 8:(c + 1) * 8],
                self.fy8[:, c * 128:(c + 1) * 128],
                self.ident_bf[0:8, 0:8])
        ins.then_inc(sem["s_pfx"], 1)
        tensor.wait_ge(sem["s_dfx"], 4)
        for c in range(CH):
            bank = self.pmA if c % 2 == 0 else self.pmB
            if c >= 2:
                tensor.wait_ge(sem["s_dfx"], c + 3)
            for icc in range(CH):
                ins = tensor.matmul(
                    bank[:, 0:8],
                    self.wkv_sb[:, icc * 2 * D + c * 128:icc * 2 * D + (c + 1) * 128],
                    self.y8c[:, icc * 8:(icc + 1) * 8],
                    start=(icc == 0), stop=(icc == CH - 1))
            ins.then_inc(sem["s_pfx"], 1)
        for c in range(CH):
            bank = self.ptkA if c % 2 == 0 else self.ptkB
            if c >= 2:
                tensor.wait_ge(sem["s_dfx"], c + 20)
            for icc in range(CH):
                ins = tensor.matmul(
                    bank[:, 0:8],
                    self.wkv_sb[:, icc * 2 * D + D + c * 128:icc * 2 * D + D + (c + 1) * 128],
                    self.y8c[:, icc * 8:(icc + 1) * 8],
                    start=(icc == 0), stop=(icc == CH - 1))
            ins.then_inc(sem["s_pfx"], 1)

    def p2tail_act(self, scalar):
        sem = self.sem
        scalar.wait_ge(sem["s_dfx"], 2)
        scalar.activation(self.scr[0:8, :], self.fx18[:, :], AF.Copy,
                          accum_out=self.st_sum[0:8, 0:1])
        scalar.drain()
        scalar.activation(self.scr[0:8, :], self.fx18[:, :], AF.Square,
                          accum_out=self.st_sq[0:8, 0:1]).then_inc(sem["s_afx"], 1)
        scalar.wait_ge(sem["s_dfx"], 21)
        scalar.activation(self.adjkc[:, :], self.adjkvc[:, :], AF.Exp).then_inc(sem["s_afx"], 1)

    def p2tail_gpsimd(self, gpsimd):
        sem = self.sem
        nt = self.nt
        gpsimd.wait_ge(sem["s_cc"], 1)
        gpsimd.wait_ge(sem["s_dkv"], nt)
        gpsimd.wait_ge(sem["s_gkv"], CH * nt)
        gpsimd.wait_ge(sem["s_kw"], 32 * nt)
        gpsimd.wait_ge(sem["s_pkv"], 32 * nt)
        self.dma(gpsimd, "s_fx8", self.fs2[:, :], self.s2_co[:, :])
        self.fx8_s2 = self.dcnt["s_fx8"]
        gpsimd.wait_ge(sem["s_fx8"], 16 * self.fx8_s2)
        gpsimd.tensor_copy(self.fs2c[:, :], self.fs2[:, :]).then_inc(sem["s_gkv"], 1)
        self.dma(gpsimd, "s_fx8", self.fx8[:, :], self.x8_in[:, :])
        self.fx8_x8 = self.dcnt["s_fx8"]
        gpsimd.wait_ge(sem["s_wcw2"], 16)
        self.wt_w2f0 = self.dcnt["s_wt"] + 1
        for b in range(4):
            if b >= 1:
                gpsimd.wait_ge(sem["s_pfx"], b)
            self.dma(gpsimd, "s_wt",
                     self.w2f[:, :].rearrange("p (k d) -> p k d", k=4),
                     self.w2_bf[b * 512:(b + 1) * 512, :].rearrange("(k p) d -> p k d", p=128))
        gpsimd.wait_ge(sem["s_dfx"], 41)
        self.dma(gpsimd, "s_fx8", self.tots_ci[:, :], self.totsb[:, :])
        gpsimd.wait_ge(sem["s_fx8"], 16 * self.dcnt["s_fx8"])
        if self.use_cc:
            gpsimd.collective_compute(
                "AllGather", ALU.bypass, replica_groups=[list(range(NCORES))],
                ins=[self.tots_ci.ap()], outs=[self.tots_co.ap()],
            ).then_inc(sem["s_cc"], 1)
        else:
            for r in range(NCORES):
                self.dma(gpsimd, "s_fx8", self.tots_co[r * 128:(r + 1) * 128, :], self.tots_ci[:, :])
            gpsimd.wait_ge(sem["s_fx8"], 16 * self.dcnt["s_fx8"])
            gpsimd.nop().then_inc(sem["s_cc"], 1)
        gpsimd.wait_ge(sem["s_pfx"], 37)
        gpsimd.wait_ge(sem["s_wcq"], 16)
        self.wt_wq0 = self.dcnt["s_wt"]
        for g in range(2, 4):
            c = self.dcnt["s_wt"]
            gpsimd.wait_ge(sem["s_wt"], 16 * c)
            self.wload_ocgroup(gpsimd, self.wq_sb, self.wq_bf, g)
            self.dcnt["s_wt"] = c + 1
        gpsimd.wait_ge(sem["s_fy"], 1)
        self.dma(gpsimd, "s_fx8", self.out8m_sp[:, :], self.fx8[:, :])
        self.fx8_om = self.dcnt["s_fx8"]
        gpsimd.wait_ge(sem["s_cc"], 2)
        self.fx8_gth0 = self.dcnt["s_fx8"] + 1
        for r in range(NCORES):
            if r >= 1:
                gpsimd.wait_ge(sem["s_dfx"], 61 + r)
            self.dma(gpsimd, "s_fx8", self.gth[:, :], self.tots_co[r * 128:(r + 1) * 128, :])
        if self.dbg:
            self.dma(gpsimd, "s_fx8", self.tots_dbg[:, :], self.tots_co[:, :])

    def p2tail_dve(self, vector):
        sem = self.sem
        banks = [self.pmA, self.pmB, self.ptkA, self.ptkB]
        vector.wait_ge(sem["s_pfx"], 4)
        for ob in range(4):
            ins = vector.tensor_copy(self.fo8[:, ob * 512:(ob + 1) * 512], banks[ob][0:8, 0:512])
        ins.then_inc(sem["s_dfx"], 1)
        vector.wait_ge(sem["s_fx8"], 16 * self.fx8_x8)
        vector.drain()
        vector.scalar_tensor_tensor(self.fx18[:, :], self.fo8[:, :], 0.5,
                                    self.fx8[:, :], ALU.mult, ALU.add).then_inc(sem["s_dfx"], 1)
        vector.wait_ge(sem["s_afx"], 1)
        vector.tensor_scalar(self.st_mean[0:8, 0:1], self.st_sum[0:8, 0:1], 1.0 / D, None, ALU.mult)
        vector.tensor_scalar(self.st_v[0:8, 0:1], self.st_sq[0:8, 0:1], 1.0 / D, None, ALU.mult)
        vector.drain()
        vector.tensor_mul(self.st_t1[0:8, 0:1], self.st_mean[0:8, 0:1], self.st_mean[0:8, 0:1])
        vector.drain()
        vector.tensor_sub(self.st_v[0:8, 0:1], self.st_v[0:8, 0:1], self.st_t1[0:8, 0:1])
        vector.drain()
        vector.tensor_scalar(self.st_v[0:8, 0:1], self.st_v[0:8, 0:1], EPS_LN, None, ALU.add)
        vector.drain()
        self.rsqrt_chain8(vector)
        vector.tensor_scalar(self.fy8[:, :], self.fx18[:, :],
                             self.st_mean[0:8, 0:1], self.st_rs[0:8, 0:1],
                             ALU.subtract, ALU.mult).then_inc(sem["s_dfx"], 1)
        vector.wait_ge(sem["s_pfx"], 5)
        for c in range(CH):
            ins = vector.tensor_copy(self.y8c[:, c * 8:(c + 1) * 8], self.ptr[:, c * 8:(c + 1) * 8])
        ins.then_inc(sem["s_dfx"], 1)
        for c in range(CH):
            bank = self.pmA if c % 2 == 0 else self.pmB
            vector.wait_ge(sem["s_pfx"], 6 + c)
            vector.tensor_copy(self.fkv8[:, c * 8:(c + 1) * 8], bank[:, 0:8]).then_inc(sem["s_dfx"], 1)
        vector.drain()
        vector.tensor_scalar_min(self.adjkvc[:, :], self.fkv8[:, :], 0.0).then_inc(sem["s_dfx"], 1)
        for c in range(CH):
            bank = self.ptkA if c % 2 == 0 else self.ptkB
            vector.wait_ge(sem["s_pfx"], 22 + c)
            vector.tensor_copy(self.fv8[:, c * 8:(c + 1) * 8], bank[:, 0:8]).then_inc(sem["s_dfx"], 1)
        vector.wait_ge(sem["s_afx"], 2)
        vector.scalar_tensor_tensor(self.fk8[:, :], self.fkv8[:, :], 0.0,
                                    self.adjkc[:, :], ALU.max, ALU.add).then_inc(sem["s_dfx"], 1)
        vector.drain()
        vector.tensor_mul(self.fkv8[:, :], self.fk8[:, :], self.fv8[:, :]).then_inc(sem["s_dfx"], 1)
        vector.drain()
        for c in range(CH):
            vector.tensor_tensor_scan(
                self.n8kvc[:, c * 8:(c + 1) * 8], self.fkv8[:, c * 8:(c + 1) * 8],
                self.fkv8[:, c * 8:(c + 1) * 8], 0.0, ALU.add, ALU.bypass)
            ins = vector.tensor_tensor_scan(
                self.n8kc[:, c * 8:(c + 1) * 8], self.fk8[:, c * 8:(c + 1) * 8],
                self.fk8[:, c * 8:(c + 1) * 8], 0.0, ALU.add, ALU.bypass)
        ins.then_inc(sem["s_dfx"], 1)
        vector.drain()
        vector.tensor_copy(self.totsb[:, 0:16], self.ck[:, :])
        vector.tensor_copy(self.totsb[:, 16:32], self.ckv[:, :]).then_inc(sem["s_dfx"], 1)
        for c in range(CH):
            bank = (self.pmA, self.pmB, self.ptkA, self.ptkB)[c % 4]
            vector.wait_ge(sem["s_pfx"], 38 + c)
            vector.tensor_copy(self.fk8[:, c * 8:(c + 1) * 8], bank[:, 0:8]).then_inc(sem["s_dfx"], 1)
        vector.drain()
        vector.tensor_scalar_min(self.fv8[:, :], self.fk8[:, :], 0.0).then_inc(sem["s_dfx"], 1)
        vector.wait_ge(sem["s_afx"], 3)
        vector.scalar_tensor_tensor(self.fv8[:, :], self.fk8[:, :], 0.0,
                                    self.fkv8[:, :], ALU.max, ALU.add).then_inc(sem["s_dfx"], 1)
        vector.drain()
        vector.tensor_scalar(self.q8m[:, :], self.fv8[:, :], self.masks_sb[:, 0:1],
                             None, ALU.mult).then_inc(sem["s_dfx"], 1)
        vector.drain()
        vector.tensor_scalar(self.fx8[:, :], self.fo8[:, :], self.masks_sb[0:8, 2:3],
                             None, ALU.mult).then_inc(sem["s_fy"], 1)

    def fixup_rounds(self, vector):
        sem = self.sem
        acc = [self.accA, self.accB]
        vector.memset(self.accA[:, :], 0.0)
        vector.drain()
        for r in range(NCORES):
            vector.wait_ge(sem["s_fx8"], 16 * (self.fx8_gth0 + r))
            src, dst = acc[r % 2], acc[(r + 1) % 2]
            if r == 0:
                vector.tensor_copy(self.adjkc[:, 0:32], self.gth[:, 32:64])
            vector.scalar_tensor_tensor(dst[:, :], self.gth[:, :],
                                        self.pmsel_sb[:, r:r + 1], src[:, :],
                                        ALU.mult, ALU.add).then_inc(sem["s_dfx"], 2 if r == 0 else 1)
            vector.drain()
        fin = acc[NCORES % 2]
        vector.drain()
        vector.tensor_sub(self.dk[:, :], bass.AP(self.n8kc, 7, [[128, 128], [8, CH]]),
                          self.adjkc[:, 0:16])
        vector.tensor_sub(self.dkv[:, :], bass.AP(self.n8kvc, 7, [[128, 128], [8, CH]]),
                          self.adjkc[:, 16:32])
        vector.drain()
        vector.scalar_tensor_tensor(self.pk[:, :], self.dk[:, :], self.masks_sb[:, 1:2],
                                    fin[:, 0:16], ALU.mult, ALU.add)
        vector.scalar_tensor_tensor(self.pkv[:, :], self.dkv[:, :], self.masks_sb[:, 1:2],
                                    fin[:, 16:32], ALU.mult, ALU.add)
        vector.drain()
        for c in range(CH):
            vector.tensor_scalar(self.adjkvc[:, c * 8:(c + 1) * 8], self.n8kvc[:, c * 8:(c + 1) * 8],
                                 self.dkv[:, c:c + 1], self.masks_sb[:, 0:1], ALU.subtract, ALU.mult)
            ins = vector.tensor_scalar(self.adjkc[:, c * 8:(c + 1) * 8], self.n8kc[:, c * 8:(c + 1) * 8],
                                       self.dk[:, c:c + 1], self.masks_sb[:, 0:1], ALU.subtract, ALU.mult)
        ins.then_inc(sem["s_dfx"], 1)

    def rsqrt_chain8(self, vector):
        w = 1
        vector.tensor_scalar(self.st_i1[0:8, 0:w], self.st_v[0:8, 0:w].bitcast(I32), 9, None, ALU.logical_shift_right)
        vector.drain()
        vector.tensor_scalar(self.st_i2[0:8, 0:w], self.st_i1[0:8, 0:w], MAGIC8, -1, ALU.subtract, ALU.mult)
        vector.drain()
        vector.tensor_scalar(self.st_i1[0:8, 0:w], self.st_i2[0:8, 0:w], 8, None, ALU.logical_shift_left)
        vector.drain()
        cur = self.st_i1[0:8, 0:w].bitcast(F32)
        for it in range(4):
            nxt = self.st_rs[0:8, 0:w] if it == 3 else (self.st_r[0:8, 0:w] if it % 2 == 0 else self.st_r[0:8, 2:2 + w])
            vector.tensor_mul(self.st_t1[0:8, 0:w], cur, cur)
            vector.drain()
            vector.tensor_mul(self.st_t2[0:8, 0:w], self.st_t1[0:8, 0:w], self.st_v[0:8, 0:w])
            vector.drain()
            vector.tensor_scalar(self.st_t1[0:8, 0:w], self.st_t2[0:8, 0:w], -0.5, 1.5, ALU.mult, ALU.add)
            vector.drain()
            vector.tensor_mul(nxt, cur, self.st_t1[0:8, 0:w])
            vector.drain()
            cur = nxt



    def p3a_gpsimd(self, gpsimd):
        sem = self.sem
        gpsimd.wait_ge(sem["s_dfx"], 70)
        gpsimd.dma_start(out=self.wo_bf[:, :], in_=self.wot_in[:, :]).then_inc(sem["s_wco"], 16)
        gpsimd.dma_start(out=self.w1_bf[:, :], in_=self.w1t_in[:, :]).then_inc(sem["s_wc13"], 16)
        gpsimd.dma_start(out=self.w3_bf[:, :], in_=self.w3t_in[:, :]).then_inc(sem["s_wc13"], 16)

    def p3a_act(self, scalar):
        sem = self.sem
        nt = self.nt
        scalar.wait_ge(sem["s_pfx"], 4)
        scalar.wait_ge(sem["s_pkv"], 32 * nt)
        for j in range(min(2, nt)):
            scalar.wait_ge(sem["s_qw"] if False else sem["s_yw"], 0)
            self.dma(scalar, "s_yr", self.ycm3[:, (j % 2) * FT:(j % 2 + 1) * FT],
                     self.y_sp[j, :, :])
        scalar.wait_ge(sem["s_dfx"], 58)
        scalar.activation(self.fkv8[:, :], self.fv8[:, :], AF.Exp).then_inc(sem["s_afx"], 1)
        for i in range(nt):
            for c in range(CH):
                scalar.wait_ge(sem["s_dmin"], CH * nt + i * CH + c + 1)
                scalar.activation(self.texp3[:, (c % 2) * TILE:(c % 2 + 1) * TILE],
                                  self.tmin3[:, (c % 2) * TILE:(c % 2 + 1) * TILE],
                                  AF.Exp).then_inc(sem["s_aex"], 1)
            if i + 2 < nt:
                scalar.wait_ge(sem["s_pq"], 16 * (i + 1))
                self.dma(scalar, "s_yr", self.ycm3[:, ((i + 2) % 2) * FT:((i + 2) % 2 + 1) * FT],
                         self.y_sp[i + 2, :, :])
            scalar.wait_ge(sem["s_eq"], 16 * (i + 1))
            self.dma(scalar, "s_qw", self.q_sp[i, :, :], self.qst[:, :])

    def p3a_tensor(self, tensor):
        sem = self.sem
        nt = self.nt
        for c in range(CH):
            bank = (self.pmA, self.pmB, self.ptkA, self.ptkB)[c % 4]
            g = c // 4
            if c % 4 == 0:
                if g < 2:
                    tensor.wait_ge(sem["s_wkv2"], 16 * (5 + g))
                else:
                    tensor.wait_ge(sem["s_wt"], 16 * (self.wt_wq0 + g - 1))
            if c == 2 or c == 3:
                tensor.wait_ge(sem["s_dfx"], 34 + c)
            if c >= 4:
                tensor.wait_ge(sem["s_dfx"], 38 + c)
            for icc in range(CH):
                ins = tensor.matmul(
                    bank[:, 0:8],
                    self.wq_sb[:, icc * D + c * 128:icc * D + (c + 1) * 128],
                    self.y8c[:, icc * 8:(icc + 1) * 8],
                    start=(icc == 0), stop=(icc == CH - 1))
            ins.then_inc(sem["s_pfx"], 1)
        tensor.wait_ge(sem["s_wkv2"], 16 * 6)
        tensor.wait_ge(sem["s_wt"], 16 * (self.wt_wq0 + 2))
        for i in range(nt):
            tensor.wait_ge(sem["s_yr"], 16 * (nt + i + 1))
            for occ in range(CH):
                k = i * CH + occ
                bank = (self.pmA, self.pmB, self.ptkA, self.ptkB)[occ % 4]
                if k < 4:
                    tensor.wait_ge(sem["s_dfx"], 54 + k)
                else:
                    tensor.wait_ge(sem["s_eq"], k - 3)
                for icc in range(CH):
                    ins = tensor.matmul(
                        bank[:, 0:TILE],
                        self.wq_sb[:, icc * D + occ * 128:icc * D + (occ + 1) * 128],
                        self.ycm3[:, (i % 2) * FT + icc * TILE:(i % 2) * FT + (icc + 1) * TILE],
                        start=(icc == 0), stop=(icc == CH - 1))
                ins.then_inc(sem["s_pq"], 1)

    def p3a_dve(self, vector):
        sem = self.sem
        nt = self.nt
        for i in range(nt):
            for c in range(CH):
                k = i * CH + c
                bank = (self.pmA, self.pmB, self.ptkA, self.ptkB)[c % 4]
                vector.wait_ge(sem["s_pq"], k + 1)
                if i == 0 and c == 0:
                    vector.wait_ge(sem["s_dfx"], 40)
                    vector.wait_ge(sem["s_gkv"], CH * nt + 1)
                    vector.wait_ge(sem["s_fx8"], 16 * self.fx8_om)
                if c == 0 and i >= 1:
                    vector.wait_ge(sem["s_qw"], 16 * i)
                vector.tensor_scalar_min(
                    self.tmin3[:, (c % 2) * TILE:(c % 2 + 1) * TILE],
                    bank[:, 0:TILE], 0.0).then_inc(sem["s_dmin"], 1)
                vector.wait_ge(sem["s_aex"], CH * nt + k + 1)
                ins = vector.scalar_tensor_tensor(
                    self.qst[:, c * TILE:(c + 1) * TILE],
                    bank[:, 0:TILE], 0.0,
                    self.texp3[:, (c % 2) * TILE:(c % 2 + 1) * TILE],
                    ALU.max, ALU.add)
                ins.then_inc(sem["s_eq"], 1)
                if c == CH - 1:
                    pass
            if i == min(2, nt - 1):
                self.fixup_rounds(vector)


    def p3b_sync(self, sync):
        sem = self.sem
        nt = self.nt
        H = FT // 2

        def kload(i):
            for h in range(2):
                if i >= 1:
                    sync.wait_ge(sem["s_dden"], 16 * (i - 1) + 8 * (h + 1))
                self.dma(sync, "s_kr", self.kvcb[:, h * H:(h + 1) * H],
                         self.kvc_sp[i, :, h * H:(h + 1) * H])
                self.dma(sync, "s_kr", self.kcb[:, h * H:(h + 1) * H],
                         self.kc_sp[i, :, h * H:(h + 1) * H])

        sync.wait_ge(sem["s_kw"], 32)
        sync.wait_ge(sem["s_pfx"], 37)
        kload(0)
        sync.wait_ge(sem["s_pq"], 16 * nt)
        sync.wait_ge(sem["s_wco"], 16)
        for g in range(4):
            self.dma(sync, "s_wkv2",
                     bass.AP(self.wo_sb, g * 512, [[self.wo_sb.shape[1], 128], [D, CH], [1, 512]]),
                     self.wo_bf[:, g * 512:(g + 1) * 512].rearrange("(k p) o -> p k o", p=128))
        sync.wait_ge(sem["s_kw"], 32 * nt)
        for i in range(1, nt):
            kload(i)

    def p3b_gpsimd(self, gpsimd):
        sem = self.sem
        nt = self.nt
        def patch_rows():
            gpsimd.wait_ge(sem["s_eao"], 4)
            c0 = self.dcnt["s_fx8"]
            gpsimd.wait_ge(sem["s_fx8"], 16 * c0)
            gpsimd.dma_start(out=self.x2b[0:8, 0:D], in_=self.out8m_sp[:, :],
                             accum_op=ALU.add).then_inc(sem["s_fx8"], 16)
            self.dcnt["s_fx8"] = c0 + 1
            self.fx8_pat = self.dcnt["s_fx8"]

        def xload(idx):
            if idx >= 3:
                gpsimd.wait_ge(sem["s_eao"], (idx - 2) * 4)
            self.dma(gpsimd, "s_x2",
                     self.xs3[:, (idx % 3) * D:(idx % 3 + 1) * D],
                     self.x_in[idx * 128:(idx + 1) * 128, :])

        gpsimd.wait_ge(sem["s_pq"], 16 * nt)
        for idx in range(min(3, S * nt)):
            xload(idx)
        for i in range(nt):
            for c in range(CH):
                gpsimd.wait_ge(sem["s_aln"], i * CH + c + 1)
                gpsimd.tensor_tensor(
                    self.ratio[:, (i % 2) * FT + c * TILE:(i % 2) * FT + (c + 1) * TILE],
                    self.numb[:, (c % 8) * TILE:(c % 8 + 1) * TILE],
                    self.recb[:, (c % 8) * TILE:(c % 8 + 1) * TILE],
                    ALU.mult).then_inc(sem["s_dr"], 1)
            if i == min(1, nt - 1):
                patch_rows()
            if i >= 1:
                for idx in (2 * i + 1, 2 * i + 2):
                    if idx < S * nt:
                        xload(idx)

    def p3b_act(self, scalar):
        sem = self.sem
        nt = self.nt
        scalar.wait_ge(sem["s_pfx"], 37)
        H = FT // 2

        def qload(j):
            scalar.wait_ge(sem["s_qw"], 16 * (j + 1))
            for h in range(2):
                if j >= 1:
                    scalar.wait_ge(sem["s_dden"], 16 * (j - 1) + 8 * (h + 1))
                self.dma(scalar, "s_qr", self.qb[:, h * H:(h + 1) * H],
                         self.q_sp[j, :, h * H:(h + 1) * H])

        def recip(j):
            for g in range(2):
                for c in range(g * 8, g * 8 + 8):
                    scalar.wait_ge(sem["s_dden"], j * CH + c + 1)
                    scalar.activation(self.lnb[:, (c % 8) * TILE:(c % 8 + 1) * TILE],
                                      self.denb[:, (c % 8) * TILE:(c % 8 + 1) * TILE],
                                      AF.Ln, bias=self.consts[:, 1:2])
                scalar.drain()
                for c in range(g * 8, g * 8 + 8):
                    scalar.activation(self.recb[:, (c % 8) * TILE:(c % 8 + 1) * TILE],
                                      self.lnb[:, (c % 8) * TILE:(c % 8 + 1) * TILE],
                                      AF.Exp, scale=self.consts[:, 2:3]).then_inc(sem["s_aln"], 1)
                scalar.drain()

        def spill_y3(j):
            scalar.wait_ge(sem["s_etp"], 2 * S * nt + (j + 1) * 2 * S)
            self.dma(scalar, "s_3w", self.y3_sp[j, :, :], self.y3cm[:, :])

        qload(0)
        recip(0)
        for i in range(nt):
            if i + 1 < nt:
                qload(i + 1)
                recip(i + 1)
            for s in range(S):
                idx = i * S + s
                scalar.wait_ge(sem["s_eao"], i * 8 + 4 * (s + 1))
                if i == 0:
                    scalar.wait_ge(sem["s_fx8"], 16 * self.fx8_pat)
                if idx:
                    scalar.wait_ge(sem["s_ast"], nt * S + idx)
                xa = self.x2b[:, s * D:(s + 1) * D]
                col = (i % 2) * S + s
                scalar.activation(self.scr[:, :], xa, AF.Copy, accum_out=self.st_sum[:, col:col + 1])
                scalar.drain()
                scalar.activation(self.scr[:, :], xa, AF.Square,
                                  accum_out=self.st_sq[:, col:col + 1]).then_inc(sem["s_ast"], 1)
            if i == 0:
                scalar.wait_ge(sem["s_eao"], 4)
                scalar.wait_ge(sem["s_fx8"], 16 * self.fx8_pat)
                self.dma(scalar, "s_ow", self.x28_sp[:, :], self.x2b[0:8, 0:D])
            scalar.wait_ge(sem["s_eao"], 8 * (i + 1))
            self.dma(scalar, "s_ow",
                     self.out[i * TILE:(i + 1) * TILE, :].rearrange("(s p) d -> p s d", p=128),
                     self.x2b[:, :].rearrange("p (s d) -> p s d", s=S))
            if i >= 1:
                spill_y3(i - 1)
        spill_y3(nt - 1)

    def p3b_tensor(self, tensor):
        sem = self.sem
        nt = self.nt

        def tp(j):
            for s in range(S):
                tensor.wait_ge(sem["s_dy"], nt * S + j * S + s + 1)
                for g in range(2):
                    kk = 2 * S * nt + (j * S + s) * 2 + g
                    if kk >= 2 * S * nt + 2:
                        tensor.wait_ge(sem["s_etp"], kk - 1)
                    for jj in range(8):
                        cj = g * 8 + jj
                        ins = tensor.transpose(
                            self.ptr[:, cj * 128:(cj + 1) * 128],
                            self.ytk3[:, s * D + cj * 128:s * D + (cj + 1) * 128],
                            self.ident_bf[:, :])
                    ins.then_inc(sem["s_ptp"], 1)

        for i in range(nt):
            tensor.wait_ge(sem["s_dr"], 16 * (i + 1))
            for s in range(S):
                for ob in range(4):
                    k = i * 8 + s * 4 + ob
                    bank = (self.ptkA, self.ptkB, self.pmA, self.pmB)[k % 4]
                    if i == 0:
                        tensor.wait_ge(sem["s_wkv2"], 16 * (7 + ob))
                    if k >= 4:
                        tensor.wait_ge(sem["s_eao"], k - 3)
                    for icc in range(CH):
                        ins = tensor.matmul(
                            bank[:, 0:512],
                            self.ratio[:, (i % 2) * FT + icc * TILE + s * 128:
                                       (i % 2) * FT + icc * TILE + s * 128 + 128],
                            self.wo_sb[:, icc * D + ob * 512:icc * D + (ob + 1) * 512],
                            start=(icc == 0), stop=(icc == CH - 1))
                    ins.then_inc(sem["s_pao"], 1)
            if i >= 1:
                tp(i - 1)
        tp(nt - 1)

    def p3b_dve(self, vector):
        sem = self.sem
        nt = self.nt

        def dens_g(j, g):
            vector.wait_ge(sem["s_qr"], 16 * (2 * j + g + 1))
            vector.wait_ge(sem["s_kr"], 16 * (4 * j + 2 * (g + 1)))
            if g == 1:
                vector.wait_ge(sem["s_dr"], j * CH + 8)
            elif j >= 1:
                vector.wait_ge(sem["s_dr"], j * CH)
            vector.drain()
            for c in range(g * 8, g * 8 + 8):
                vector.scalar_tensor_tensor(
                    self.denb[:, (c % 8) * TILE:(c % 8 + 1) * TILE],
                    self.kcb[:, c * TILE:(c + 1) * TILE], self.pk[:, c:c + 1],
                    self.qb[:, c * TILE:(c + 1) * TILE], ALU.add, ALU.mult)
                vector.scalar_tensor_tensor(
                    self.numb[:, (c % 8) * TILE:(c % 8 + 1) * TILE],
                    self.kvcb[:, c * TILE:(c + 1) * TILE], self.pkv[:, c:c + 1],
                    self.qb[:, c * TILE:(c + 1) * TILE], ALU.add, ALU.mult).then_inc(sem["s_dden"], 1)

        def evacs(j):
            if j >= 1:
                vector.wait_ge(sem["s_3w"], 16 * j)
            for s in range(S):
                for g in range(2):
                    kk = 2 * S * nt + (j * S + s) * 2 + g
                    vector.wait_ge(sem["s_ptp"], kk + 1)
                    for jj in range(8):
                        cj = g * 8 + jj
                        ins = vector.tensor_copy(
                            self.y3cm[:, cj * TILE + s * 128:cj * TILE + s * 128 + 128],
                            self.ptr[:, cj * 128:(cj + 1) * 128])
                    ins.then_inc(sem["s_etp"], 1)

        for i in range(nt):
            if i == 0:
                vector.wait_ge(sem["s_qr"], 32)
                vector.wait_ge(sem["s_kr"], 64)
                vector.wait_ge(sem["s_x2"], 0)
                vector.wait_ge(sem["s_dfx"], 70)
                tmp = bass.AP(self.fk8, 0, [[128, 128], [8, CH], [1, 8]])
                for (buf, adj) in ((self.kvcb, self.adjkvc), (self.kcb, self.adjkc),
                                   (self.qb, self.q8m)):
                    bv = bass.AP(buf, 0, [[buf.shape[1], 128], [TILE, CH], [1, 8]])
                    av = bass.AP(adj, 0, [[128, 128], [8, CH], [1, 8]])
                    vector.scalar_tensor_tensor(tmp, bv, self.masks_sb[:, 3:4], av,
                                                ALU.mult, ALU.add)
                    vector.drain()
                    ins = vector.tensor_copy(bv, tmp)
                    vector.drain()
                ins.then_inc(sem["s_fy"], 1)
                dens_g(0, 0)
                dens_g(0, 1)
            nxt = i + 1 < nt
            if nxt:
                if i >= 1:
                    vector.wait_ge(sem["s_pao"], 8 * i)
                dens_g(i + 1, 0)
            if i == 0:
                vector.wait_ge(sem["s_qw"], 16 * nt)
            if i >= 1:
                vector.wait_ge(sem["s_ow"], 16 * (i + 1))
                vector.drain()
            for s in range(S):
                for ob in range(4):
                    k = i * 8 + s * 4 + ob
                    bank = (self.ptkA, self.ptkB, self.pmA, self.pmB)[k % 4]
                    vector.wait_ge(sem["s_pao"], k + 1)
                    idx = i * S + s
                    vector.wait_ge(sem["s_x2"], 16 * (idx + 1))
                    vector.scalar_tensor_tensor(
                        self.x2b[:, s * D + ob * 512:s * D + (ob + 1) * 512],
                        bank[:, 0:512], ATT_SCALE,
                        self.xs3[:, (idx % 3) * D + ob * 512:(idx % 3) * D + (ob + 1) * 512],
                        ALU.mult, ALU.add).then_inc(sem["s_eao"], 1)
                if s == 0 and nxt:
                    dens_g(i + 1, 1)
            if i >= 1:
                evacs(i - 1)
            vector.wait_ge(sem["s_ast"], nt * S + i * S + S)
            xs_ = [self.x2b[:, s * D:(s + 1) * D] for s in range(S)]
            yd = [self.ytk3[:, s * D:(s + 1) * D] for s in range(S)]
            self.ln_finalize_apply(vector, xs_, yd, "s_dy", None, b=(i % 2) * S)
        evacs(nt - 1)


    def p4_gpsimd(self, gpsimd):
        sem = self.sem
        nt = self.nt
        gpsimd.wait_ge(sem["s_pao"], 8 * nt)
        gpsimd.wait_ge(sem["s_dden"], 16 * nt)
        gpsimd.wait_ge(sem["s_dr"], 16 * nt)
        gpsimd.wait_ge(sem["s_eao"], 8 * nt)
        gpsimd.wait_ge(sem["s_wc13"], 32)
        self.wt_w140 = self.dcnt["s_wt"]
        for g in range(4):
            c = self.dcnt["s_wt"]
            gpsimd.wait_ge(sem["s_wt"], 16 * c)
            self.wload_ocgroup(gpsimd, self.w1_sb, self.w1_bf, g)
            self.dcnt["s_wt"] = c + 1
        for g in range(4):
            c = self.dcnt["s_wt"]
            gpsimd.wait_ge(sem["s_wt"], 16 * c)
            self.wload_ocgroup(gpsimd, self.w3_sb, self.w3_bf, g)
            self.dcnt["s_wt"] = c + 1
        gpsimd.wait_ge(sem["s_ps2"], 4)
        self.dma(gpsimd, "s_fx8", self.s2b_ci[:, :], self.s2sb[:, :])
        gpsimd.wait_ge(sem["s_fx8"], 16 * self.dcnt["s_fx8"])
        if self.use_cc:
            gpsimd.collective_compute(
                "AllReduce", ALU.add, replica_groups=[list(range(NCORES))],
                ins=[self.s2b_ci.ap()], outs=[self.s2b_co.ap()],
            ).then_inc(sem["s_cc"], 1)
        else:
            self.dma(gpsimd, "s_fx8", self.s2b_co[:, :], self.s2b_ci[:, :])
            gpsimd.wait_ge(sem["s_fx8"], 16 * self.dcnt["s_fx8"])
            gpsimd.nop().then_inc(sem["s_cc"], 1)
        self.p5_gpsimd(gpsimd)

    def p4_act(self, scalar):
        sem = self.sem
        nt = self.nt
        b_lg = S * nt
        b_h1 = CH * nt
        b_h2 = 8 * nt
        b_es2 = S * nt - 1
        scalar.wait_ge(sem["s_eao"], 8 * nt)
        scalar.wait_ge(sem["s_ow"], 16 * self.dcnt["s_ow"])
        for j in range(min(2, nt)):
            scalar.wait_ge(sem["s_3w"], 16 * (j + 1))
            self.dma(scalar, "s_3r", self.ycm4[:, (j % 2) * FT:(j % 2 + 1) * FT],
                     self.y3_sp[j, :, :])
        for i in range(nt):
            for s in range(S):
                idx = i * S + s
                scalar.wait_ge(sem["s_dd"], b_lg + idx + 1)
                scalar.activation(self.rsl[:, s:s + 1], self.rd[:, s:s + 1], AF.Silu).then_inc(sem["s_asl"], 1)
            for occ in range(CH):
                k = i * CH + occ
                scalar.wait_ge(sem["s_ph1"], b_h1 + k + 1)
                bank = self.pmA if occ % 2 == 0 else self.pmB
                scalar.activation(self.h14[:, occ * TILE:(occ + 1) * TILE],
                                  bank[:, 0:TILE], AF.Silu).then_inc(sem["s_eh1"], 1)
            for s in range(S):
                idx = i * S + s
                if idx >= 2:
                    scalar.wait_ge(sem["s_es2"], b_es2 + idx - 1)
                for ob in range(4):
                    k = idx * 4 + ob
                    scalar.wait_ge(sem["s_ph2"], b_h2 + k + 1)
                    bank = self.ptkA if k % 2 == 0 else self.ptkB
                    scalar.activation(
                        self.h2t4[:, (idx % 2) * D + ob * 512:(idx % 2) * D + (ob + 1) * 512],
                        bank[:, 0:512], AF.Silu).then_inc(sem["s_eh2"], 1)
            if i + 2 < nt:
                scalar.wait_ge(sem["s_3w"], 16 * (i + 3))
                scalar.wait_ge(sem["s_ph1"], b_h1 + (i + 1) * CH)
                self.dma(scalar, "s_3r", self.ycm4[:, ((i + 2) % 2) * FT:((i + 2) % 2 + 1) * FT],
                         self.y3_sp[i + 2, :, :])
        self.p5_act(scalar)

    def p4_dve(self, vector):
        sem = self.sem
        nt = self.nt
        b_lg = S * nt
        for i in range(nt):
            for s in range(S):
                self.router_dve_p4(vector, i, s, b_lg)
        vector.wait_ge(sem["s_ps2"], 3)
        vector.tensor_copy(self.s2sb[:, :], self.ps2[:, 0:128]).then_inc(sem["s_ps2"], 1)
        self.p5_dve(vector)

    def router_dve_p4(self, vector, i, s, base):
        sem = self.sem
        idx = i * S + s
        L = self.lg[:, s * 8:(s + 1) * 8]
        T1 = self.rt1[:, s * 8:(s + 1) * 8]
        T2 = self.rt2[:, s * 8:(s + 1) * 8]
        vector.wait_ge(sem["s_plg"], base + idx + 1)
        vector.tensor_copy(L, self.pmisc[:, s * 8:(s + 1) * 8]).then_inc(sem["s_elg"], 1)
        vector.drain()
        vector.reduce_max(self.rm1[:, s:s + 1], L, axis=AX.X)
        vector.drain()
        vector.tensor_scalar(T1, L, self.rm1[:, s:s + 1], None, ALU.is_ge)
        vector.drain()
        vector.scalar_tensor_tensor(T2, T1, -1e30, L, ALU.mult, ALU.add)
        vector.drain()
        vector.reduce_max(self.rm2[:, s:s + 1], T2, axis=AX.X)
        vector.drain()
        vector.tensor_sub(self.rd[:, s:s + 1], self.rm1[:, s:s + 1], self.rm2[:, s:s + 1])
        vector.drain()
        vector.tensor_scalar(self.rd[:, s:s + 1], self.rd[:, s:s + 1], 1e-30, None, ALU.add).then_inc(sem["s_dd"], 1)
        vector.tensor_scalar(T2, T2, self.rm2[:, s:s + 1], None, ALU.is_ge)
        vector.wait_ge(sem["s_asl"], base + idx + 1)
        vector.reciprocal(self.rg1[:, s:s + 1], self.rd[:, s:s + 1])
        vector.drain()
        vector.tensor_mul(self.rg1[:, s:s + 1], self.rg1[:, s:s + 1], self.rsl[:, s:s + 1])
        vector.drain()
        vector.tensor_scalar(self.rg2[:, s:s + 1], self.rg1[:, s:s + 1], -1.0, 1.0, ALU.mult, ALU.add)
        vector.drain()
        vector.tensor_scalar(T1, T1, self.rg1[:, s:s + 1], None, ALU.mult)
        vector.drain()
        vector.scalar_tensor_tensor(self.G[:, s * 8:(s + 1) * 8], T2, self.rg2[:, s:s + 1],
                                    T1, ALU.mult, ALU.add).then_inc(sem["s_dg"], 1)

    def p4_tensor(self, tensor):
        sem = self.sem
        nt = self.nt
        b_lg = S * nt
        b_h1 = CH * nt
        b_h2 = 8 * nt
        b_es2 = S * nt - 1
        b_etp = 4 * S * nt
        tensor.wait_ge(sem["s_eao"], 8 * nt)
        for i in range(nt):
            tensor.wait_ge(sem["s_3r"], 16 * (i + 1))
            for s in range(S):
                idx = i * S + s
                if b_lg + idx >= 1:
                    tensor.wait_ge(sem["s_elg"], b_lg + idx)
                for icc in range(CH):
                    ins = tensor.matmul(
                        self.pmisc[:, s * 8:(s + 1) * 8],
                        self.ycm4[:, (i % 2) * FT + icc * TILE + s * 128:
                                 (i % 2) * FT + icc * TILE + s * 128 + 128],
                        self.wrt_sb[:, icc * 8:(icc + 1) * 8],
                        start=(icc == 0), stop=(icc == CH - 1))
                ins.then_inc(sem["s_plg"], 1)
            for occ in range(CH):
                k = i * CH + occ
                bank = self.pmA if occ % 2 == 0 else self.pmB
                if b_h1 + k >= 2:
                    tensor.wait_ge(sem["s_eh1"], b_h1 + k - 1)
                if i == 0:
                    tensor.wait_ge(sem["s_wt"], 16 * (self.wt_w140 + occ // 4 + 1))
                for icc in range(CH):
                    ins = tensor.matmul(
                        bank[:, 0:TILE],
                        self.w1_sb[:, icc * D + occ * 128:icc * D + (occ + 1) * 128],
                        self.ycm4[:, (i % 2) * FT + icc * TILE:(i % 2) * FT + (icc + 1) * TILE],
                        start=(icc == 0), stop=(icc == CH - 1))
                ins.then_inc(sem["s_ph1"], 1)
            for s in range(S):
                idx = i * S + s
                tensor.wait_ge(sem["s_eh1"], b_h1 + (i + 1) * CH)
                for ob in range(4):
                    k = idx * 4 + ob
                    bank = self.ptkA if k % 2 == 0 else self.ptkB
                    if b_h2 + k >= 2:
                        tensor.wait_ge(sem["s_eh2"], b_h2 + k - 1)
                    if i == 0:
                        tensor.wait_ge(sem["s_wt"], 16 * (self.wt_w140 + 4 + ob + 1))
                    for icc in range(CH):
                        ins = tensor.matmul(
                            bank[:, 0:512],
                            self.h14[:, icc * TILE + s * 128:icc * TILE + s * 128 + 128],
                            self.w3_sb[:, icc * D + ob * 512:icc * D + (ob + 1) * 512],
                            start=(icc == 0), stop=(icc == CH - 1))
                    ins.then_inc(sem["s_ph2"], 1)
                tensor.wait_ge(sem["s_eh2"], b_h2 + (idx + 1) * 4)
                tensor.wait_ge(sem["s_dg"], b_lg + idx + 1)
                first = (i == 0 and s == 0)
                last = (i == nt - 1 and s == S - 1)
                for c in range(CH):
                    ins = tensor.matmul(
                        self.ps2[:, c * 8:(c + 1) * 8],
                        self.h2t4[:, (idx % 2) * D + c * 128:(idx % 2) * D + (c + 1) * 128],
                        self.G[:, s * 8:(s + 1) * 8],
                        start=(first and c == 0), stop=(last and c == CH - 1))
                if last:
                    ins.then_inc(sem["s_ps2"], 1)
                else:
                    ins.then_inc(sem["s_es2"], 1)
        self.p5_tensor(tensor)


    def p5_gpsimd(self, gpsimd):
        sem = self.sem
        gpsimd.wait_ge(sem["s_cc"], 3)
        self.dma(gpsimd, "s_fx8", self.fs2b[:, :], self.s2b_co[:, :])
        gpsimd.wait_ge(sem["s_fx8"], 16 * self.dcnt["s_fx8"])
        gpsimd.tensor_copy(self.fs2c[:, :], self.fs2b[:, :]).then_inc(sem["s_gkv"], 2)
        self.dma(gpsimd, "s_fx8", self.fx28[:, :], self.x28_sp[:, :])
        self.fx8_x28 = self.dcnt["s_fx8"]
        self.wt_w250 = self.dcnt["s_wt"] + 1
        for b in range(4):
            if b >= 1:
                gpsimd.wait_ge(sem["s_pfx"], 53 + b)
            self.dma(gpsimd, "s_wt",
                     self.w2s5[:, :].rearrange("p (k d) -> p k d", k=4),
                     self.w2_bf[b * 512:(b + 1) * 512, :].rearrange("(k p) d -> p k d", p=128))

    def p5_tensor(self, tensor):
        sem = self.sem
        banks = [self.pmA, self.pmB, self.ptkA, self.ptkB]
        tensor.wait_ge(sem["s_gkv"], CH * self.nt + 3)
        for b in range(4):
            tensor.wait_ge(sem["s_wt"], 16 * (self.wt_w250 + b))
            for il in range(4):
                icc = b * 4 + il
                for ob in range(4):
                    ins = tensor.matmul(
                        banks[ob][0:8, 0:512],
                        self.fs2c[:, icc * 8:(icc + 1) * 8],
                        self.w2s5[:, il * D + ob * 512:il * D + (ob + 1) * 512],
                        start=(icc == 0), stop=(icc == CH - 1))
            ins.then_inc(sem["s_pfx"], 1)

    def p5_dve(self, vector):
        sem = self.sem
        banks = [self.pmA, self.pmB, self.ptkA, self.ptkB]
        vector.wait_ge(sem["s_pfx"], 57)
        for ob in range(4):
            ins = vector.tensor_copy(self.fo8b[:, ob * 512:(ob + 1) * 512], banks[ob][0:8, 0:512])
        ins.then_inc(sem["s_dfx"], 1)
        vector.wait_ge(sem["s_fx8"], 16 * self.fx8_x28)
        vector.drain()
        vector.scalar_tensor_tensor(self.fo8b[:, :], self.fo8b[:, :],
                                    self.masks_sb[0:8, 2:3], self.fx28[:, :],
                                    ALU.mult, ALU.add).then_inc(sem["s_dfx"], 1)

    def p5_act(self, scalar):
        sem = self.sem
        scalar.wait_ge(sem["s_dfx"], 72)
        self.dma(scalar, "s_ow", self.out[0:8, :], self.fo8b[:, :])
        scalar.wait_ge(sem["s_ow"], 16 * self.dcnt["s_ow"])
        scalar.wait_ge(sem["s_3w"], 16 * self.nt)


    def build(self):
        from contextlib import ExitStack
        nc = self.nc
        with ExitStack() as ctx:
            self.declare()
            self.alloc_sbuf()
            self.semaphores(ctx)
            block = ctx.enter_context(nc.Block())
            mk = self
            sem = self.sem

            @block.sync
            def _(sync):
                sync.wait_ge(sem["s_init"], 1)
                mk.p1_sync(sync)
                mk.rest_sync(sync)

            @block.gpsimd
            def _(gpsimd):
                mk.p1_gpsimd(gpsimd)
                mk.rest_gpsimd(gpsimd)

            @block.scalar
            def _(scalar):
                scalar.wait_ge(sem["s_wt"], 16 * 4)
                scalar.wait_ge(sem["s_init"], 1)
                mk.p1_act(scalar)
                mk.rest_act(scalar)

            @block.vector
            def _(vector):
                vector.wait_ge(sem["s_init"], 1)
                mk.p1_dve(vector)
                mk.rest_dve(vector)

            @block.tensor
            def _(tensor):
                tensor.wait_ge(sem["s_wt"], 16 * 4)
                tensor.wait_ge(sem["s_init"], 1)
                mk.p1_tensor(tensor)
                mk.rest_tensor(tensor)

        return nc


    def rest_sync(self, sync):
        if self.stop_after >= 2:
            self.p2_sync(sync)
        if self.stop_after >= 3:
            self.p3b_sync(sync)

    def rest_gpsimd(self, gpsimd):
        sem = self.sem
        if self.stop_after >= 2:
            self.p2_gpsimd(gpsimd)
        if self.stop_after >= 3:
            self.p3a_gpsimd(gpsimd)
            self.p3b_gpsimd(gpsimd)
        if self.stop_after >= 4:
            self.p4_gpsimd(gpsimd)
        if self.dbg:
            gpsimd.wait_ge(sem["s_cc"], 1)
            self.dma(gpsimd, "s_fx8", self.s2_dbg[:, :], self.s2_co[:, :])
            gpsimd.wait_ge(sem["s_fx8"], 16 * self.dcnt["s_fx8"])

    def rest_act(self, scalar):
        if self.stop_after >= 2:
            self.p2_act(scalar)
        if self.stop_after >= 3:
            self.p3a_act(scalar)
            self.p3b_act(scalar)
        if self.stop_after >= 4:
            self.p4_act(scalar)
        if self.stop_after == 2:
            scalar.wait_ge(self.sem["s_kw"], 32 * self.nt)
            scalar.wait_ge(self.sem["s_dfx"], 43 + NCORES)
        if self.stop_after == 3:
            scalar.wait_ge(self.sem["s_ow"], 16 * self.dcnt["s_ow"])
            scalar.wait_ge(self.sem["s_3w"], 16 * self.nt)
        scalar.wait_ge(self.sem["s_yw"], 16 * self.nt)

    def rest_dve(self, vector):
        if self.stop_after >= 2:
            self.p2_dve(vector)
        if self.stop_after >= 3:
            self.p3a_dve(vector)
            self.p3b_dve(vector)
        if self.stop_after >= 4:
            self.p4_dve(vector)

    def rest_tensor(self, tensor):
        if self.stop_after >= 2:
            self.p2_tensor(tensor)
        if self.stop_after >= 3:
            self.p3a_tensor(tensor)
            self.p3b_tensor(tensor)
        if self.stop_after >= 4:
            self.p4_tensor(tensor)



def make_core_inputs(x_shard, x8, wq, wkv, wo, w1, w3, w2, wr, core):
    masks = np.zeros((128, 4), dtype=np.float32)
    if core == 0:
        masks[:, 0] = 1.0
        masks[:, 2] = 0.5
    masks[:, 3] = 1.0 - masks[:, 0]
    if core in (0, 1):
        masks[:, 1] = 1.0
    pmsel = np.zeros((128, 8), dtype=np.float32)
    if core % 2 == 1:
        pmsel[:, core - 1] = 1.0
    return {
        "x": x_shard, "x8": x8,
        "wqt": wq, "wkvt": wkv, "wot": wo,
        "w1t": w1, "w3t": w3, "w2t": w2, "wrt": wr,
        "ident": np.eye(128, dtype=np.float32),
        "masks": masks, "pmsel": pmsel,
    }


def prep_weights(W_qkv, W_out, w1, w3, w2, wr):
    wq = np.ascontiguousarray(np.asarray(W_qkv)[0:D].T.astype(np.float32))
    wkv = np.ascontiguousarray(np.asarray(W_qkv)[D:3 * D].T.astype(np.float32))
    wo = np.ascontiguousarray(np.asarray(W_out).T.astype(np.float32))
    w1t = np.ascontiguousarray(np.asarray(w1).T.astype(np.float32))
    w3t = np.ascontiguousarray(np.asarray(w3).T.astype(np.float32))
    w2t = np.ascontiguousarray(np.asarray(w2).T.astype(np.float32))
    wrt = np.ascontiguousarray(np.asarray(wr).T.astype(np.float32)
                               .reshape(16, 128, 8).transpose(1, 0, 2).reshape(128, 128))
    return wq, wkv, wo, w1t, w3t, w2t, wrt


def kernel(L0_attn_x, L0_attn_W_qkv, L0_attn_W_out, L0_ffn_w1, L0_ffn_w3,
           L0_ffn_w2, L0_ffn_wr):
    x = np.ascontiguousarray(np.asarray(L0_attn_x, dtype=np.float32)).reshape(-1, D)
    wq, wkv, wo, w1t, w3t, w2t, wrt = prep_weights(
        L0_attn_W_qkv, L0_attn_W_out, L0_ffn_w1, L0_ffn_w3, L0_ffn_w2, L0_ffn_wr)
    x8 = np.ascontiguousarray(x[0:8])
    shards = [np.ascontiguousarray(x[c * TPC:(c + 1) * TPC]) for c in range(NCORES)]
    mk = MK()
    nc = mk.build()
    in_maps = [
        make_core_inputs(shards[c], x8, wq, wkv, wo, w1t, w3t, w2t, wrt, c)
        for c in range(NCORES)
    ]
    res = run_bass_kernel_spmd(nc, in_maps, core_ids=list(range(NCORES)))
    out = np.concatenate([r["out"] for r in res.results], axis=0)
    return out.reshape(4, 8192, D)


# version 12
# speedup vs baseline: 1.0597x; 1.0597x over previous
import numpy as np
import concourse.bass as bass
import concourse.mybir as mybir
from concourse.bass_utils import run_bass_kernel_spmd

F32 = mybir.dt.float32
BF16 = mybir.dt.bfloat16
I32 = mybir.dt.int32
AF = mybir.ActivationFunctionType
ALU = mybir.AluOpType
AX = mybir.AxisListType

NCORES = 8
D = 2048
CH = 16
TILE = 256
S = TILE // 128
TPC = 4096
NT_FULL = TPC // TILE
FT = CH * TILE
ATT_SCALE = 128.0 ** -0.5
EPS_LN = 1e-5
EPS_DIV = 1e-9
MAGIC8 = 0x5F3759DF >> 8


class MK:
    def __init__(self, nt=NT_FULL, debug_outputs=False, use_cc=True, stop_after=99):
        self.nt = nt
        self.dbg = debug_outputs
        self.use_cc = use_cc
        self.stop_after = stop_after
        self.nc = bass.Bass("TRN2", target_bir_lowering=False)
        from collections import defaultdict
        self.dcnt = defaultdict(int)

    def wload_ocgroup(self, eng, sb, dram, g, ncol=D):
        out = bass.AP(sb, g * 512, [[sb.shape[1], 128], [ncol, CH], [1, 512]])
        in_ = dram[:, g * 512:(g + 1) * 512].rearrange("(k p) o -> p k o", p=128)
        eng.dma_start(out=out, in_=in_).then_inc(self.sem["s_wt"], 16)

    def dma(self, eng, semname, out, in_):
        sem = self.sem[semname]
        c = self.dcnt[semname]
        if c:
            eng.wait_ge(sem, 16 * c)
        ins = eng.dma_start(out=out, in_=in_)
        ins.then_inc(sem, 16)
        self.dcnt[semname] = c + 1
        return ins


    def declare(self):
        nc = self.nc
        nt = self.nt
        ntok = nt * TILE
        self.x_in = nc.dram_tensor("x", [ntok, D], F32, kind="ExternalInput")
        self.x8_in = nc.dram_tensor("x8", [8, D], F32, kind="ExternalInput")
        self.wqt_in = nc.dram_tensor("wqt", [D, D], F32, kind="ExternalInput")
        self.wkvt_in = nc.dram_tensor("wkvt", [D, 2 * D], F32, kind="ExternalInput")
        self.wot_in = nc.dram_tensor("wot", [D, D], F32, kind="ExternalInput")
        self.w1t_in = nc.dram_tensor("w1t", [D, D], F32, kind="ExternalInput")
        self.w3t_in = nc.dram_tensor("w3t", [D, D], F32, kind="ExternalInput")
        self.w2t_in = nc.dram_tensor("w2t", [D, D], F32, kind="ExternalInput")
        self.wrt_in = nc.dram_tensor("wrt", [128, CH * 8], F32, kind="ExternalInput")
        self.ident_in = nc.dram_tensor("ident", [128, 128], F32, kind="ExternalInput")
        self.masks_in = nc.dram_tensor("masks", [128, 4], F32, kind="ExternalInput")
        self.pmsel_in = nc.dram_tensor("pmsel", [128, 8], F32, kind="ExternalInput")
        self.out = nc.dram_tensor("out", [ntok, D], F32, kind="ExternalOutput")

        sk = "ExternalOutput" if self.dbg else "Internal"
        self.y_sp = nc.dram_tensor("y_sp", [nt, 128, FT], BF16, kind=sk)
        self.q_sp = nc.dram_tensor("q_sp", [nt, 128, FT], BF16, kind=sk)
        self.kvc_sp = nc.dram_tensor("kvc_sp", [nt, 128, FT], F32, kind=sk)
        self.kc_sp = nc.dram_tensor("kc_sp", [nt, 128, FT], F32, kind=sk)
        self.y3_sp = nc.dram_tensor("y3_sp", [nt, 128, FT], BF16, kind=sk)
        if self.dbg:
            self.s2_dbg = nc.dram_tensor("s2_dbg", [128, 128], F32, kind="ExternalOutput")
            self.tots_dbg = nc.dram_tensor("tots_dbg", [128 * NCORES, 64], F32, kind="ExternalOutput")
        self.out8m_sp = nc.dram_tensor("out8m_sp", [8, D], F32, kind="Internal")
        self.x28_sp = nc.dram_tensor("x28_sp", [8, D], F32, kind="Internal")
        self.s2_ci = nc.dram_tensor("s2_ci", [128, 128], F32, kind="Internal")
        self.s2_co = nc.dram_tensor("s2_co", [128, 128], F32, kind="Internal")
        self.tots_ci = nc.dram_tensor("tots_ci", [128, 64], F32, kind="Internal")
        self.tots_co = nc.dram_tensor("tots_co", [128 * NCORES, 64], F32, kind="Internal")
        self.s2b_ci = nc.dram_tensor("s2b_ci", [128, 128], F32, kind="Internal")
        self.s2b_co = nc.dram_tensor("s2b_co", [128, 128], F32, kind="Internal")
        self.wkv_bf = nc.dram_tensor("wkv_bf", [D, 2 * D], BF16, kind="Internal")
        self.wq_bf = nc.dram_tensor("wq_bf", [D, D], BF16, kind="Internal")
        self.wo_bf = nc.dram_tensor("wo_bf", [D, D], BF16, kind="Internal")
        self.w1_bf = nc.dram_tensor("w1_bf", [D, D], BF16, kind="Internal")
        self.w3_bf = nc.dram_tensor("w3_bf", [D, D], BF16, kind="Internal")
        self.w2_bf = nc.dram_tensor("w2_bf", [D, D], BF16, kind="Internal")

    def alloc_sbuf(self):
        nc = self.nc
        A = nc.alloc_sbuf_tensor_at
        BASE = 16512
        RW1 = BASE
        RW2 = BASE + 65536
        B = BASE + 131072
        Pb = B + 73728

        self.w1_sb = A("w1_sb", [128, CH * D], BF16, offset=RW1)
        self.w3_sb = A("w3_sb", [128, CH * D], BF16, offset=RW2)
        self.wkv_sb = A("wkv_sb", [128, CH * 2 * D], BF16, offset=RW1)
        self.wq_sb = A("wq_sb", [128, CH * D], BF16, offset=RW1)
        self.w2s_sb = A("w2s_sb", [128, CH * D], BF16, offset=RW2)
        self.wo_sb = A("wo_sb", [128, CH * D], BF16, offset=RW1)

        self.xsl = A("xsl", [128, 3 * D], F32, offset=B)
        self.ytk = A("ytk", [128, 2 * D], BF16, offset=B + 24576)
        self.ycm = A("ycm", [128, 2 * FT], BF16, offset=B + 32768)
        self.h1 = A("h1", [128, FT], BF16, offset=B + 49152)
        self.h2t = A("h2t", [128, 2 * D], BF16, offset=B + 57344)
        self.scr = A("scr", [128, D], F32, offset=B + 65536)

        self.ycm2 = A("ycm2", [128, 2 * FT], BF16, offset=B)
        self.kch = A("kch", [128, 3 * TILE], F32, offset=B + 16384)
        self.vch = A("vch", [128, 3 * TILE], F32, offset=B + 19456)
        self.kvch = A("kvch", [128, 3 * TILE], F32, offset=B + 22528)
        self.tmin = A("tmin", [128, 3 * TILE], F32, offset=B + 25600)
        self.texp = A("texp", [128, 3 * TILE], F32, offset=B + 28672)
        self.kvcst = A("kvcst", [128, FT], F32, offset=B + 31744)
        self.kcst = A("kcst", [128, FT], F32, offset=B + 48128)
        self.w2f = A("w2f", [128, 4 * D], BF16, offset=B)

        FB = B + 16384
        self.fs2 = A("fs2", [128, 128], F32, offset=FB)
        self.fx8 = A("fx8", [8, D], F32, offset=FB + 512)
        self.fo8 = A("fo8", [8, D], F32, offset=FB + 8704)
        self.fx18 = A("fx18", [8, D], F32, offset=FB + 16896)
        self.fy8 = A("fy8", [8, D], BF16, offset=FB + 25088)

        self.ycm3 = A("ycm3", [128, 2 * FT], BF16, offset=B)
        self.qst = A("qst", [128, FT], BF16, offset=B + 16384)
        self.tmin3 = A("tmin3", [128, 4 * TILE], F32, offset=B + 24576)
        self.texp3 = A("texp3", [128, 4 * TILE], F32, offset=B + 28672)

        C = RW2
        self.qb = A("qb", [128, FT], BF16, offset=C)
        self.kvcb = A("kvcb", [128, FT], F32, offset=C + 8192)
        self.kcb = A("kcb", [128, FT], F32, offset=C + 24576)
        self.ratio = A("ratio", [128, 2 * FT], BF16, offset=C + 40960)
        self.xs3 = A("xs3", [128, 3 * D], F32, offset=C + 57344)
        self.x2b = A("x2b", [128, S * D], F32, offset=C + 81920)
        self.ytk3 = A("ytk3", [128, 2 * D], BF16, offset=C + 98304)
        self.y3cm = A("y3cm", [128, FT], BF16, offset=C + 106496)
        self.numb = A("numb", [128, 8 * TILE], F32, offset=C + 114688)
        self.denb = A("denb", [128, 8 * TILE], F32, offset=C + 122880)
        self.lnb = A("lnb", [128, 8 * TILE], F32, offset=C + 131072)
        self.recb = self.denb

        self.ycm4 = A("ycm4", [128, 2 * FT], BF16, offset=B)
        self.h14 = A("h14", [128, FT], BF16, offset=B + 16384)
        self.h2t4 = A("h2t4", [128, 2 * D], BF16, offset=B + 24576)

        self.fs2b = A("fs2b", [128, 128], F32, offset=B + 40960)
        self.fo8b = A("fo8b", [8, D], F32, offset=B + 41472)
        self.fx28 = A("fx28", [8, D], F32, offset=B + 49664)
        self.w2s5 = A("w2s5", [128, 4 * D], BF16, offset=B + 16384)

        p = Pb
        def per(name, shape, dtype):
            nonlocal p
            t = A(name, shape, dtype, offset=p)
            sz = int(np.prod(shape[1:])) * mybir.dt.size(dtype)
            p += (sz + 31) // 32 * 32
            return t
        self.consts = per("consts", [128, 8], F32)
        self.magic = per("magic", [128, 1], I32)
        self.ident_bf = per("ident_bf", [128, 128], BF16)
        self.wrt_sb = per("wrt_sb", [128, CH * 8], BF16)
        self.masks_sb = per("masks_sb", [128, 4], F32)
        self.pmsel_sb = per("pmsel_sb", [128, 8], F32)
        self.ckv = per("ckv", [128, 16], F32)
        self.ck = per("ck", [128, 16], F32)
        self.st_sum = per("st_sum", [128, 4], F32)
        self.st_sq = per("st_sq", [128, 4], F32)
        self.st_mean = per("st_mean", [128, 2], F32)
        self.st_t1 = per("st_t1", [128, 2], F32)
        self.st_t2 = per("st_t2", [128, 2], F32)
        self.st_v = per("st_v", [128, 2], F32)
        self.st_i1 = per("st_i1", [128, 2], I32)
        self.st_i2 = per("st_i2", [128, 2], I32)
        self.st_r = per("st_r", [128, 4], F32)
        self.st_rs = per("st_rs", [128, 2], F32)
        self.lg = per("lg", [128, 16], F32)
        self.rt1 = per("rt1", [128, 16], F32)
        self.rt2 = per("rt2", [128, 16], F32)
        self.rm1 = per("rm1", [128, 2], F32)
        self.rm2 = per("rm2", [128, 2], F32)
        self.rd = per("rd", [128, 2], F32)
        self.rg1 = per("rg1", [128, 2], F32)
        self.rg2 = per("rg2", [128, 2], F32)
        self.rsl = per("rsl", [128, 2], F32)
        self.G = per("G", [128, 16], BF16)
        self.s2sb = per("s2sb", [128, 128], F32)
        self.totsb = per("totsb", [128, 64], F32)
        self.n8kvc = per("n8kvc", [128, 128], F32)
        self.n8kc = per("n8kc", [128, 128], F32)
        self.dkv = per("dkv", [128, 16], F32)
        self.dk = per("dk", [128, 16], F32)
        self.pkv = per("pkv", [128, 16], F32)
        self.pk = per("pk", [128, 16], F32)
        self.adjkvc = per("adjkvc", [128, 128], F32)
        self.adjkc = per("adjkc", [128, 128], F32)
        self.q8m = per("q8m", [128, 128], BF16)
        self.y8c = per("y8c", [128, 128], BF16)
        self.fs2c = per("fs2c", [128, 128], BF16)
        self.fk8 = per("fk8", [128, 128], F32)
        self.fv8 = per("fv8", [128, 128], F32)
        self.fkv8 = per("fkv8", [128, 128], F32)
        self.gth = per("gth", [128, 64], F32)
        self.accA = per("accA", [128, 64], F32)
        self.accB = per("accB", [128, 64], F32)
        assert p <= 229376, f"persistent overflow {p}"

        P = nc.alloc_psum_tensor
        self.pmA = P("pmA", [128, 512], F32)
        self.pmB = P("pmB", [128, 512], F32)
        self.ptkA = P("ptkA", [128, 512], F32)
        self.ptkB = P("ptkB", [128, 512], F32)
        self.ptr = P("ptr", [128, 2048], BF16)
        self.ps2 = P("ps2", [128, 512], F32)
        self.pmisc = P("pmisc", [128, 512], F32)

    def semaphores(self, ctx):
        names = [
            "s_x", "s_yw", "s_yr", "s_qw", "s_qr", "s_kw", "s_kr",
            "s_3w", "s_3r", "s_ow", "s_wt", "s_fx8", "s_cc",
            "s_ptp", "s_plg", "s_ph1", "s_ph2", "s_pkv", "s_pq", "s_pao",
            "s_pfx", "s_ps2", "s_es2",
            "s_etp", "s_elg", "s_eh1", "s_eh2", "s_ekv", "s_eq", "s_eao",
            "s_ast", "s_aex", "s_aln", "s_asl", "s_dmin", "s_dd", "s_dden",
            "s_dy", "s_dg", "s_dkv", "s_dr", "s_dx2", "s_dfx", "s_afx", "s_gkv",
            "s_fx", "s_fy", "s_fkv", "s_fq", "s_f3", "s_init", "s_x2",
            "s_wc", "s_wcw2", "s_wcq", "s_wco", "s_wc13", "s_wkv2",
        ]
        self.sem = {}
        for n in names:
            self.sem[n] = ctx.enter_context(self.nc.semaphore(n))


    def rsqrt_chain(self, vector, v_ap, out_ap, w):
        vector.tensor_scalar(self.st_i1[:, 0:w], v_ap.bitcast(I32), 9, None, ALU.logical_shift_right)
        vector.drain()
        vector.tensor_scalar(self.st_i2[:, 0:w], self.st_i1[:, 0:w], MAGIC8, -1, ALU.subtract, ALU.mult)
        vector.drain()
        vector.tensor_scalar(self.st_i1[:, 0:w], self.st_i2[:, 0:w], 8, None, ALU.logical_shift_left)
        vector.drain()
        cur = self.st_i1[:, 0:w].bitcast(F32)
        for it in range(4):
            nxt = out_ap if it == 3 else (self.st_r[:, 0:w] if it % 2 == 0 else self.st_r[:, 2:2 + w])
            vector.tensor_mul(self.st_t1[:, 0:w], cur, cur)
            vector.drain()
            vector.tensor_mul(self.st_t2[:, 0:w], self.st_t1[:, 0:w], v_ap)
            vector.drain()
            vector.tensor_scalar(self.st_t1[:, 0:w], self.st_t2[:, 0:w], -0.5, 1.5, ALU.mult, ALU.add)
            vector.drain()
            vector.tensor_mul(nxt, cur, self.st_t1[:, 0:w])
            vector.drain()
            cur = nxt

    def ln_finalize_apply(self, vector, x_slices, ytk_dst, sem_dy, sem_fx, b=0):
        sem = self.sem
        vector.tensor_scalar(self.st_mean[:, 0:S], self.st_sum[:, b:b + S], 1.0 / D, None, ALU.mult)
        vector.tensor_scalar(self.st_v[:, 0:S], self.st_sq[:, b:b + S], 1.0 / D, None, ALU.mult)
        vector.drain()
        vector.tensor_mul(self.st_t1[:, 0:S], self.st_mean[:, 0:S], self.st_mean[:, 0:S])
        vector.drain()
        vector.tensor_sub(self.st_v[:, 0:S], self.st_v[:, 0:S], self.st_t1[:, 0:S])
        vector.drain()
        vector.tensor_scalar(self.st_v[:, 0:S], self.st_v[:, 0:S], EPS_LN, None, ALU.add)
        vector.drain()
        self.rsqrt_chain(vector, self.st_v[:, 0:S], self.st_rs[:, 0:S], S)
        for s in range(S):
            vector.tensor_scalar(
                ytk_dst[s], x_slices[s],
                self.st_mean[:, s:s + 1], self.st_rs[:, s:s + 1],
                ALU.subtract, ALU.mult).then_inc(sem[sem_dy], 1)

    def router_dve(self, vector, i, s):
        sem = self.sem
        idx = i * S + s
        L = self.lg[:, s * 8:(s + 1) * 8]
        T1 = self.rt1[:, s * 8:(s + 1) * 8]
        T2 = self.rt2[:, s * 8:(s + 1) * 8]
        vector.wait_ge(sem["s_plg"], idx + 1)
        vector.tensor_copy(L, self.pmisc[:, s * 8:(s + 1) * 8]).then_inc(sem["s_elg"], 1)
        vector.drain()
        vector.reduce_max(self.rm1[:, s:s + 1], L, axis=AX.X)
        vector.drain()
        vector.tensor_scalar(T1, L, self.rm1[:, s:s + 1], None, ALU.is_ge)
        vector.drain()
        vector.scalar_tensor_tensor(T2, T1, -1e30, L, ALU.mult, ALU.add)
        vector.drain()
        vector.reduce_max(self.rm2[:, s:s + 1], T2, axis=AX.X)
        vector.drain()
        vector.tensor_sub(self.rd[:, s:s + 1], self.rm1[:, s:s + 1], self.rm2[:, s:s + 1])
        vector.drain()
        vector.tensor_scalar(self.rd[:, s:s + 1], self.rd[:, s:s + 1], 1e-30, None, ALU.add).then_inc(sem["s_dd"], 1)
        vector.tensor_scalar(T2, T2, self.rm2[:, s:s + 1], None, ALU.is_ge)
        vector.wait_ge(sem["s_asl"], idx + 1)
        vector.reciprocal(self.rg1[:, s:s + 1], self.rd[:, s:s + 1])
        vector.drain()
        vector.tensor_mul(self.rg1[:, s:s + 1], self.rg1[:, s:s + 1], self.rsl[:, s:s + 1])
        vector.drain()
        vector.tensor_scalar(self.rg2[:, s:s + 1], self.rg1[:, s:s + 1], -1.0, 1.0, ALU.mult, ALU.add)
        vector.drain()
        vector.tensor_scalar(T1, T1, self.rg1[:, s:s + 1], None, ALU.mult)
        vector.drain()
        vector.scalar_tensor_tensor(self.G[:, s * 8:(s + 1) * 8], T2, self.rg2[:, s:s + 1],
                                    T1, ALU.mult, ALU.add).then_inc(sem["s_dg"], 1)


    def p1_sync(self, sync):
        sem = self.sem
        for i in range(self.nt):
            for s in range(S):
                idx = i * S + s
                if idx >= 3:
                    sync.wait_ge(sem["s_dy"], idx - 2)
                self.dma(sync, "s_x",
                         self.xsl[:, (idx % 3) * D:(idx % 3 + 1) * D],
                         self.x_in[idx * 128:(idx + 1) * 128, :])

    def p1_gpsimd(self, gpsimd):
        sem = self.sem
        gpsimd.memset(self.consts[:, 0:1], EPS_LN)
        gpsimd.memset(self.consts[:, 1:2], EPS_DIV)
        gpsimd.memset(self.consts[:, 2:3], -1.0)
        gpsimd.memset(self.magic[:, :], MAGIC8)
        gpsimd.memset(self.ckv[:, :], 0.0)
        gpsimd.memset(self.ck[:, :], 0.0).then_inc(sem["s_init"], 1)
        gpsimd.dma_start(out=self.ident_bf[:, :], in_=self.ident_in[:, :]).then_inc(sem["s_wt"], 16)
        gpsimd.dma_start(out=self.masks_sb[:, :], in_=self.masks_in[:, :]).then_inc(sem["s_wt"], 16)
        gpsimd.dma_start(out=self.pmsel_sb[:, :], in_=self.pmsel_in[:, :]).then_inc(sem["s_wt"], 16)
        gpsimd.dma_start(
            out=self.wrt_sb[:, :], in_=self.wrt_in[:, :],
        ).then_inc(sem["s_wt"], 16)
        for g in range(4):
            gpsimd.wait_ge(sem["s_wt"], 16 * (4 + g))
            self.wload_ocgroup(gpsimd, self.w1_sb, self.w1t_in, g)
        for g in range(4):
            gpsimd.wait_ge(sem["s_wt"], 16 * (8 + g))
            self.wload_ocgroup(gpsimd, self.w3_sb, self.w3t_in, g)
        self.dcnt["s_wt"] = 12
        gpsimd.dma_start(out=self.wkv_bf[:, :], in_=self.wkvt_in[:, :]).then_inc(sem["s_wc"], 16)
        gpsimd.dma_start(out=self.w2_bf[:, :], in_=self.w2t_in[:, :]).then_inc(sem["s_wcw2"], 16)
        gpsimd.dma_start(out=self.wq_bf[:, :], in_=self.wqt_in[:, :]).then_inc(sem["s_wcq"], 16)
        gpsimd.wait_ge(sem["s_ps2"], 2)
        self.dma(gpsimd, "s_fx8", self.s2_ci[:, :], self.s2sb[:, :])
        gpsimd.wait_ge(sem["s_fx8"], 16 * self.dcnt["s_fx8"])
        if self.use_cc:
            gpsimd.collective_compute(
                "AllReduce", ALU.add, replica_groups=[list(range(NCORES))],
                ins=[self.s2_ci.ap()], outs=[self.s2_co.ap()],
            ).then_inc(sem["s_cc"], 1)
        else:
            self.dma(gpsimd, "s_fx8", self.s2_co[:, :], self.s2_ci[:, :])
            gpsimd.wait_ge(sem["s_fx8"], 16 * self.dcnt["s_fx8"])
            gpsimd.nop().then_inc(sem["s_cc"], 1)

    def p1_act(self, scalar):
        sem = self.sem

        def stats(j):
            for s in range(S):
                idx = j * S + s
                scalar.wait_ge(sem["s_x"], 16 * (idx + 1))
                if idx:
                    scalar.wait_ge(sem["s_ast"], idx)
                xa = self.xsl[:, (idx % 3) * D:(idx % 3 + 1) * D]
                col = (j % 2) * S + s
                scalar.activation(self.scr[:, :], xa, AF.Copy, accum_out=self.st_sum[:, col:col + 1])
                scalar.drain()
                scalar.activation(self.scr[:, :], xa, AF.Square,
                                  accum_out=self.st_sq[:, col:col + 1]).then_inc(sem["s_ast"], 1)

        stats(0)
        for i in range(self.nt):
            if i >= 1:
                self.p1_scalar_dma(scalar, i - 1)
            for s in range(S):
                scalar.wait_ge(sem["s_dd"], i * S + s + 1)
                scalar.activation(self.rsl[:, s:s + 1], self.rd[:, s:s + 1], AF.Silu).then_inc(sem["s_asl"], 1)
            for occ in range(CH):
                k = i * CH + occ
                scalar.wait_ge(sem["s_ph1"], k + 1)
                bank = self.pmA if occ % 2 == 0 else self.pmB
                scalar.activation(self.h1[:, occ * TILE:(occ + 1) * TILE],
                                  bank[:, 0:TILE], AF.Silu).then_inc(sem["s_eh1"], 1)
            if i + 1 < self.nt:
                stats(i + 1)
            for s in range(S):
                idx = i * S + s
                if idx >= 2:
                    scalar.wait_ge(sem["s_es2"], idx - 1)
                for ob in range(4):
                    k = idx * 4 + ob
                    scalar.wait_ge(sem["s_ph2"], k + 1)
                    bank = self.ptkA if k % 2 == 0 else self.ptkB
                    scalar.activation(
                        self.h2t[:, (idx % 2) * D + ob * 512:(idx % 2) * D + (ob + 1) * 512],
                        bank[:, 0:512], AF.Silu).then_inc(sem["s_eh2"], 1)
        self.p1_scalar_dma(scalar, self.nt - 1)

    def p1_dve(self, vector):
        sem = self.sem
        for i in range(self.nt):
            vector.wait_ge(sem["s_ast"], i * S + S)
            xs = [self.xsl[:, ((i * S + s) % 3) * D:((i * S + s) % 3 + 1) * D] for s in range(S)]
            yd = [self.ytk[:, s * D:(s + 1) * D] for s in range(S)]
            self.ln_finalize_apply(vector, xs, yd, "s_dy", "s_fx", b=(i % 2) * S)
            for s in range(S):
                for g in range(2):
                    k = (i * S + s) * 2 + g
                    vector.wait_ge(sem["s_ptp"], k + 1)
                    for j in range(8):
                        c = g * 8 + j
                        ins = vector.tensor_copy(
                            self.ycm[:, (i % 2) * FT + c * TILE + s * 128:
                                     (i % 2) * FT + c * TILE + s * 128 + 128],
                            self.ptr[:, (g * 8 + j) * 128:(g * 8 + j + 1) * 128])
                    ins.then_inc(sem["s_etp"], 1)
            for s in range(S):
                self.router_dve(vector, i, s)
        vector.wait_ge(sem["s_ps2"], 1)
        vector.tensor_copy(self.s2sb[:, :], self.ps2[:, 0:128]).then_inc(sem["s_ps2"], 1)

    def p1_tensor(self, tensor):
        sem = self.sem
        nt = self.nt
        for i in range(nt):
            for s in range(S):
                idx = i * S + s
                tensor.wait_ge(sem["s_dy"], idx + 1)
                if i >= 2:
                    tensor.wait_ge(sem["s_yw"], 16 * (i - 1))
                for g in range(2):
                    k = idx * 2 + g
                    if k >= 2:
                        tensor.wait_ge(sem["s_etp"], k - 1)
                    for j in range(8):
                        ins = tensor.transpose(
                            self.ptr[:, (g * 8 + j) * 128:(g * 8 + j + 1) * 128],
                            self.ytk[:, s * D + (g * 8 + j) * 128:s * D + (g * 8 + j + 1) * 128],
                            self.ident_bf[:, :])
                    ins.then_inc(sem["s_ptp"], 1)
            for s in range(S):
                idx = i * S + s
                tensor.wait_ge(sem["s_etp"], (idx + 1) * 2)
                if idx >= 1:
                    tensor.wait_ge(sem["s_elg"], idx)
                for icc in range(CH):
                    ins = tensor.matmul(
                        self.pmisc[:, s * 8:(s + 1) * 8],
                        self.ycm[:, (i % 2) * FT + icc * TILE + s * 128:
                                 (i % 2) * FT + icc * TILE + s * 128 + 128],
                        self.wrt_sb[:, icc * 8:(icc + 1) * 8],
                        start=(icc == 0), stop=(icc == CH - 1))
                ins.then_inc(sem["s_plg"], 1)
            for occ in range(CH):
                k = i * CH + occ
                bank = self.pmA if occ % 2 == 0 else self.pmB
                if k >= 2:
                    tensor.wait_ge(sem["s_eh1"], k - 1)
                if i == 0:
                    tensor.wait_ge(sem["s_wt"], 16 * (5 + occ // 4))
                for icc in range(CH):
                    ins = tensor.matmul(
                        bank[:, 0:TILE],
                        self.w1_sb[:, icc * D + occ * 128:icc * D + (occ + 1) * 128],
                        self.ycm[:, (i % 2) * FT + icc * TILE:(i % 2) * FT + (icc + 1) * TILE],
                        start=(icc == 0), stop=(icc == CH - 1))
                ins.then_inc(sem["s_ph1"], 1)
            for s in range(S):
                idx = i * S + s
                tensor.wait_ge(sem["s_eh1"], (i + 1) * CH)
                for ob in range(4):
                    k = idx * 4 + ob
                    bank = self.ptkA if k % 2 == 0 else self.ptkB
                    if k >= 2:
                        tensor.wait_ge(sem["s_eh2"], k - 1)
                    if i == 0:
                        tensor.wait_ge(sem["s_wt"], 16 * (9 + ob))
                    for icc in range(CH):
                        ins = tensor.matmul(
                            bank[:, 0:512],
                            self.h1[:, icc * TILE + s * 128:icc * TILE + s * 128 + 128],
                            self.w3_sb[:, icc * D + ob * 512:icc * D + (ob + 1) * 512],
                            start=(icc == 0), stop=(icc == CH - 1))
                    ins.then_inc(sem["s_ph2"], 1)
                tensor.wait_ge(sem["s_eh2"], (idx + 1) * 4)
                tensor.wait_ge(sem["s_dg"], idx + 1)
                first = (i == 0 and s == 0)
                last = (i == nt - 1 and s == S - 1)
                for c in range(CH):
                    ins = tensor.matmul(
                        self.ps2[:, c * 8:(c + 1) * 8],
                        self.h2t[:, (idx % 2) * D + c * 128:(idx % 2) * D + (c + 1) * 128],
                        self.G[:, s * 8:(s + 1) * 8],
                        start=(first and c == 0), stop=(last and c == CH - 1))
                if last:
                    ins.then_inc(sem["s_ps2"], 1)
                else:
                    ins.then_inc(sem["s_es2"], 1)

    def p1_scalar_dma(self, scalar, i):
        sem = self.sem
        scalar.wait_ge(sem["s_etp"], (i + 1) * S * 2)
        self.dma(scalar, "s_yw", self.y_sp[i, :, :],
                 self.ycm[:, (i % 2) * FT:(i % 2 + 1) * FT])


    def b_ast(self):
        return self.nt * S

    def p2_yload(self, scalar, j):
        sem = self.sem
        if j >= 2:
            scalar.wait_ge(sem["s_pkv"], 32 * (j - 1))
        scalar.wait_ge(sem["s_yw"], 16 * (j + 1))
        self.dma(scalar, "s_yr", self.ycm2[:, (j % 2) * FT:(j % 2 + 1) * FT],
                 self.y_sp[j, :, :])

    def p2_sync(self, sync):
        sem = self.sem
        sync.wait_ge(sem["s_ps2"], 2)
        sync.wait_ge(sem["s_wc"], 16)
        for g in range(4, 8):
            out = bass.AP(self.wkv_sb, g * 512, [[self.wkv_sb.shape[1], 128], [2 * D, CH], [1, 512]])
            in_ = self.wkv_bf[:, g * 512:(g + 1) * 512].rearrange("(k p) o -> p k o", p=128)
            self.dma(sync, "s_wkv2", out, in_)
        for i in range(self.nt):
            sync.wait_ge(sem["s_dkv"], i + 1)
            self.dma(sync, "s_kw", self.kvc_sp[i, :, :], self.kvcst[:, :])
            self.dma(sync, "s_kw", self.kc_sp[i, :, :], self.kcst[:, :])
        sync.wait_ge(sem["s_pfx"], 37)
        sync.wait_ge(sem["s_wcq"], 16)
        for g in range(2):
            self.dma(sync, "s_wkv2",
                     bass.AP(self.wq_sb, g * 512, [[self.wq_sb.shape[1], 128], [D, CH], [1, 512]]),
                     self.wq_bf[:, g * 512:(g + 1) * 512].rearrange("(k p) o -> p k o", p=128))

    def p2_tensor(self, tensor):
        sem = self.sem
        tensor.wait_ge(sem["s_ps2"], 2)
        tensor.wait_ge(sem["s_eh2"], 8 * self.nt)
        for i in range(self.nt):
            tensor.wait_ge(sem["s_yr"], 16 * (i + 1))
            for c in range(CH):
                kbank = self.pmA if c % 2 == 0 else self.ptkA
                vbank = self.pmB if c % 2 == 0 else self.ptkB
                n = i * 32 + 2 * c
                if i == 0:
                    tensor.wait_ge(sem["s_wt"], 16 * (self.wt_wkv0 + c // 4 + 1))
                    tensor.wait_ge(sem["s_wkv2"], 16 * (c // 4 + 1))
                if n - 3 > 0:
                    tensor.wait_ge(sem["s_ekv"], n - 3)
                for icc in range(CH):
                    ins = tensor.matmul(
                        kbank[:, 0:TILE],
                        self.wkv_sb[:, icc * 2 * D + c * 128:icc * 2 * D + (c + 1) * 128],
                        self.ycm2[:, (i % 2) * FT + icc * TILE:(i % 2) * FT + (icc + 1) * TILE],
                        start=(icc == 0), stop=(icc == CH - 1))
                ins.then_inc(sem["s_pkv"], 1)
                if n - 2 > 0:
                    tensor.wait_ge(sem["s_ekv"], n - 2)
                for icc in range(CH):
                    ins = tensor.matmul(
                        vbank[:, 0:TILE],
                        self.wkv_sb[:, icc * 2 * D + D + c * 128:icc * 2 * D + D + (c + 1) * 128],
                        self.ycm2[:, (i % 2) * FT + icc * TILE:(i % 2) * FT + (icc + 1) * TILE],
                        start=(icc == 0), stop=(icc == CH - 1))
                ins.then_inc(sem["s_pkv"], 1)
        self.p2tail_tensor(tensor)

    def p2_act(self, scalar):
        sem = self.sem
        scalar.wait_ge(sem["s_dy"], 2 * self.nt)
        for j in range(min(2, self.nt)):
            self.p2_yload(scalar, j)
        for i in range(self.nt):
            for c in range(CH):
                scalar.wait_ge(sem["s_dmin"], i * CH + c + 1)
                scalar.activation(self.texp[:, (c % 3) * TILE:(c % 3 + 1) * TILE],
                                  self.tmin[:, (c % 3) * TILE:(c % 3 + 1) * TILE],
                                  AF.Exp).then_inc(sem["s_aex"], 1)
            if i + 2 < self.nt:
                self.p2_yload(scalar, i + 2)
        self.p2tail_act(scalar)

    def p2_gpsimd(self, gpsimd):
        sem = self.sem
        gpsimd.wait_ge(sem["s_wc"], 16 * 1)
        wkv0 = self.dcnt["s_wt"]
        for g in range(4):
            c = self.dcnt["s_wt"]
            gpsimd.wait_ge(sem["s_wt"], 16 * c)
            out = bass.AP(self.wkv_sb, g * 512, [[self.wkv_sb.shape[1], 128], [2 * D, CH], [1, 512]])
            in_ = self.wkv_bf[:, g * 512:(g + 1) * 512].rearrange("(k p) o -> p k o", p=128)
            gpsimd.dma_start(out=out, in_=in_).then_inc(sem["s_wt"], 16)
            self.dcnt["s_wt"] = c + 1
        self.wt_wkv0 = wkv0
        for i in range(self.nt):
            for c in range(CH):
                gpsimd.wait_ge(sem["s_ekv"], i * 32 + 2 * c + 2)
                gpsimd.tensor_tensor(
                    self.kvch[:, (c % 3) * TILE:(c % 3 + 1) * TILE],
                    self.kch[:, (c % 3) * TILE:(c % 3 + 1) * TILE],
                    self.vch[:, (c % 3) * TILE:(c % 3 + 1) * TILE],
                    ALU.mult).then_inc(sem["s_gkv"], 1)
        self.p2tail_gpsimd(gpsimd)

    def p2_dve(self, vector):
        sem = self.sem
        nt = self.nt
        vector.wait_ge(sem["s_yw"], 16 * nt)
        vector.wait_ge(sem["s_eh2"], 8 * nt)
        for i in range(nt):
            if i >= 1:
                vector.wait_ge(sem["s_kw"], 32 * i)
            for cc in range(CH + 2):
                if cc < CH:
                    c = cc
                    n = i * 32 + 2 * c
                    kbank = self.pmA if c % 2 == 0 else self.ptkA
                    vector.wait_ge(sem["s_pkv"], n + 1)
                    vector.tensor_scalar_min(
                        self.tmin[:, (c % 3) * TILE:(c % 3 + 1) * TILE],
                        kbank[:, 0:TILE], 0.0).then_inc(sem["s_dmin"], 1)
                if 1 <= cc <= CH:
                    c = cc - 1
                    n = i * 32 + 2 * c
                    kbank = self.pmA if c % 2 == 0 else self.ptkA
                    vbank = self.pmB if c % 2 == 0 else self.ptkB
                    vector.wait_ge(sem["s_aex"], i * CH + c + 1)
                    vector.scalar_tensor_tensor(
                        self.kch[:, (c % 3) * TILE:(c % 3 + 1) * TILE],
                        kbank[:, 0:TILE], 0.0,
                        self.texp[:, (c % 3) * TILE:(c % 3 + 1) * TILE],
                        ALU.max, ALU.add).then_inc(sem["s_ekv"], 1)
                    vector.wait_ge(sem["s_pkv"], n + 2)
                    vector.tensor_copy(
                        self.vch[:, (c % 3) * TILE:(c % 3 + 1) * TILE],
                        vbank[:, 0:TILE]).then_inc(sem["s_ekv"], 1)
                if cc >= 2:
                    c = cc - 2
                    vector.wait_ge(sem["s_gkv"], i * CH + c + 1)
                    vector.tensor_tensor_scan(
                        self.kvcst[:, c * TILE:(c + 1) * TILE],
                        self.kvch[:, (c % 3) * TILE:(c % 3 + 1) * TILE],
                        self.kvch[:, (c % 3) * TILE:(c % 3 + 1) * TILE],
                        self.ckv[:, c:c + 1], ALU.add, ALU.bypass)
                    vector.tensor_tensor_scan(
                        self.kcst[:, c * TILE:(c + 1) * TILE],
                        self.kch[:, (c % 3) * TILE:(c % 3 + 1) * TILE],
                        self.kch[:, (c % 3) * TILE:(c % 3 + 1) * TILE],
                        self.ck[:, c:c + 1], ALU.add, ALU.bypass)
            vector.drain()
            for c in range(CH):
                vector.tensor_copy(self.ckv[:, c:c + 1],
                                   self.kvcst[:, c * TILE + TILE - 1:c * TILE + TILE])
                ins = vector.tensor_copy(self.ck[:, c:c + 1],
                                         self.kcst[:, c * TILE + TILE - 1:c * TILE + TILE])
            if i == 0:
                vector.drain()
                vector.tensor_copy(self.totsb[:, 32:48],
                                   bass.AP(self.kcst, 7, [[FT, 128], [TILE, CH]]))
                ins = vector.tensor_copy(self.totsb[:, 48:64],
                                         bass.AP(self.kvcst, 7, [[FT, 128], [TILE, CH]]))
            ins.then_inc(sem["s_dkv"], 1)
        self.p2tail_dve(vector)


    def p2tail_tensor(self, tensor):
        sem = self.sem
        nt = self.nt
        banks = [self.pmA, self.pmB, self.ptkA, self.ptkB]
        tensor.wait_ge(sem["s_ekv"], 32 * nt)
        tensor.wait_ge(sem["s_gkv"], CH * nt + 1)
        for b in range(4):
            tensor.wait_ge(sem["s_wt"], 16 * (self.wt_w2f0 + b))
            for il in range(4):
                icc = b * 4 + il
                for ob in range(4):
                    ins = tensor.matmul(
                        banks[ob][0:8, 0:512],
                        self.fs2c[:, icc * 8:(icc + 1) * 8],
                        self.w2f[:, il * D + ob * 512:il * D + (ob + 1) * 512],
                        start=(icc == 0), stop=(icc == CH - 1))
            ins.then_inc(sem["s_pfx"], 1)
        tensor.wait_ge(sem["s_dfx"], 3)
        for c in range(CH):
            ins = tensor.transpose(
                self.ptr[:, c * 8:(c + 1) * 8],
                self.fy8[:, c * 128:(c + 1) * 128],
                self.ident_bf[0:8, 0:8])
        ins.then_inc(sem["s_pfx"], 1)
        tensor.wait_ge(sem["s_dfx"], 4)
        for c in range(CH):
            bank = self.pmA if c % 2 == 0 else self.pmB
            if c >= 2:
                tensor.wait_ge(sem["s_dfx"], c + 3)
            for icc in range(CH):
                ins = tensor.matmul(
                    bank[:, 0:8],
                    self.wkv_sb[:, icc * 2 * D + c * 128:icc * 2 * D + (c + 1) * 128],
                    self.y8c[:, icc * 8:(icc + 1) * 8],
                    start=(icc == 0), stop=(icc == CH - 1))
            ins.then_inc(sem["s_pfx"], 1)
        for c in range(CH):
            bank = self.ptkA if c % 2 == 0 else self.ptkB
            if c >= 2:
                tensor.wait_ge(sem["s_dfx"], c + 20)
            for icc in range(CH):
                ins = tensor.matmul(
                    bank[:, 0:8],
                    self.wkv_sb[:, icc * 2 * D + D + c * 128:icc * 2 * D + D + (c + 1) * 128],
                    self.y8c[:, icc * 8:(icc + 1) * 8],
                    start=(icc == 0), stop=(icc == CH - 1))
            ins.then_inc(sem["s_pfx"], 1)

    def p2tail_act(self, scalar):
        sem = self.sem
        scalar.wait_ge(sem["s_dfx"], 2)
        scalar.activation(self.scr[0:8, :], self.fx18[:, :], AF.Copy,
                          accum_out=self.st_sum[0:8, 0:1])
        scalar.drain()
        scalar.activation(self.scr[0:8, :], self.fx18[:, :], AF.Square,
                          accum_out=self.st_sq[0:8, 0:1]).then_inc(sem["s_afx"], 1)
        scalar.wait_ge(sem["s_dfx"], 21)
        scalar.activation(self.adjkc[:, :], self.adjkvc[:, :], AF.Exp).then_inc(sem["s_afx"], 1)

    def p2tail_gpsimd(self, gpsimd):
        sem = self.sem
        nt = self.nt
        gpsimd.wait_ge(sem["s_cc"], 1)
        gpsimd.wait_ge(sem["s_dkv"], nt)
        gpsimd.wait_ge(sem["s_gkv"], CH * nt)
        gpsimd.wait_ge(sem["s_kw"], 32 * nt)
        gpsimd.wait_ge(sem["s_pkv"], 32 * nt)
        self.dma(gpsimd, "s_fx8", self.fs2[:, :], self.s2_co[:, :])
        self.fx8_s2 = self.dcnt["s_fx8"]
        gpsimd.wait_ge(sem["s_fx8"], 16 * self.fx8_s2)
        gpsimd.tensor_copy(self.fs2c[:, :], self.fs2[:, :]).then_inc(sem["s_gkv"], 1)
        self.dma(gpsimd, "s_fx8", self.fx8[:, :], self.x8_in[:, :])
        self.fx8_x8 = self.dcnt["s_fx8"]
        gpsimd.wait_ge(sem["s_wcw2"], 16)
        self.wt_w2f0 = self.dcnt["s_wt"] + 1
        for b in range(4):
            if b >= 1:
                gpsimd.wait_ge(sem["s_pfx"], b)
            self.dma(gpsimd, "s_wt",
                     self.w2f[:, :].rearrange("p (k d) -> p k d", k=4),
                     self.w2_bf[b * 512:(b + 1) * 512, :].rearrange("(k p) d -> p k d", p=128))
        gpsimd.wait_ge(sem["s_dfx"], 41)
        self.dma(gpsimd, "s_fx8", self.tots_ci[:, :], self.totsb[:, :])
        gpsimd.wait_ge(sem["s_fx8"], 16 * self.dcnt["s_fx8"])
        if self.use_cc:
            gpsimd.collective_compute(
                "AllGather", ALU.bypass, replica_groups=[list(range(NCORES))],
                ins=[self.tots_ci.ap()], outs=[self.tots_co.ap()],
            ).then_inc(sem["s_cc"], 1)
        else:
            for r in range(NCORES):
                self.dma(gpsimd, "s_fx8", self.tots_co[r * 128:(r + 1) * 128, :], self.tots_ci[:, :])
            gpsimd.wait_ge(sem["s_fx8"], 16 * self.dcnt["s_fx8"])
            gpsimd.nop().then_inc(sem["s_cc"], 1)
        gpsimd.wait_ge(sem["s_pfx"], 37)
        gpsimd.wait_ge(sem["s_wcq"], 16)
        self.wt_wq0 = self.dcnt["s_wt"]
        for g in range(2, 4):
            c = self.dcnt["s_wt"]
            gpsimd.wait_ge(sem["s_wt"], 16 * c)
            self.wload_ocgroup(gpsimd, self.wq_sb, self.wq_bf, g)
            self.dcnt["s_wt"] = c + 1
        gpsimd.wait_ge(sem["s_fy"], 1)
        self.dma(gpsimd, "s_fx8", self.out8m_sp[:, :], self.fx8[:, :])
        self.fx8_om = self.dcnt["s_fx8"]
        gpsimd.wait_ge(sem["s_cc"], 2)
        self.fx8_gth0 = self.dcnt["s_fx8"] + 1
        for r in range(NCORES):
            if r >= 1:
                gpsimd.wait_ge(sem["s_dfx"], 61 + r)
            self.dma(gpsimd, "s_fx8", self.gth[:, :], self.tots_co[r * 128:(r + 1) * 128, :])
        if self.dbg:
            self.dma(gpsimd, "s_fx8", self.tots_dbg[:, :], self.tots_co[:, :])

    def p2tail_dve(self, vector):
        sem = self.sem
        banks = [self.pmA, self.pmB, self.ptkA, self.ptkB]
        vector.wait_ge(sem["s_pfx"], 4)
        for ob in range(4):
            ins = vector.tensor_copy(self.fo8[:, ob * 512:(ob + 1) * 512], banks[ob][0:8, 0:512])
        ins.then_inc(sem["s_dfx"], 1)
        vector.wait_ge(sem["s_fx8"], 16 * self.fx8_x8)
        vector.drain()
        vector.scalar_tensor_tensor(self.fx18[:, :], self.fo8[:, :], 0.5,
                                    self.fx8[:, :], ALU.mult, ALU.add).then_inc(sem["s_dfx"], 1)
        vector.wait_ge(sem["s_afx"], 1)
        vector.tensor_scalar(self.st_mean[0:8, 0:1], self.st_sum[0:8, 0:1], 1.0 / D, None, ALU.mult)
        vector.tensor_scalar(self.st_v[0:8, 0:1], self.st_sq[0:8, 0:1], 1.0 / D, None, ALU.mult)
        vector.drain()
        vector.tensor_mul(self.st_t1[0:8, 0:1], self.st_mean[0:8, 0:1], self.st_mean[0:8, 0:1])
        vector.drain()
        vector.tensor_sub(self.st_v[0:8, 0:1], self.st_v[0:8, 0:1], self.st_t1[0:8, 0:1])
        vector.drain()
        vector.tensor_scalar(self.st_v[0:8, 0:1], self.st_v[0:8, 0:1], EPS_LN, None, ALU.add)
        vector.drain()
        self.rsqrt_chain8(vector)
        vector.tensor_scalar(self.fy8[:, :], self.fx18[:, :],
                             self.st_mean[0:8, 0:1], self.st_rs[0:8, 0:1],
                             ALU.subtract, ALU.mult).then_inc(sem["s_dfx"], 1)
        vector.wait_ge(sem["s_pfx"], 5)
        for c in range(CH):
            ins = vector.tensor_copy(self.y8c[:, c * 8:(c + 1) * 8], self.ptr[:, c * 8:(c + 1) * 8])
        ins.then_inc(sem["s_dfx"], 1)
        for c in range(CH):
            bank = self.pmA if c % 2 == 0 else self.pmB
            vector.wait_ge(sem["s_pfx"], 6 + c)
            vector.tensor_copy(self.fkv8[:, c * 8:(c + 1) * 8], bank[:, 0:8]).then_inc(sem["s_dfx"], 1)
        vector.drain()
        vector.tensor_scalar_min(self.adjkvc[:, :], self.fkv8[:, :], 0.0).then_inc(sem["s_dfx"], 1)
        for c in range(CH):
            bank = self.ptkA if c % 2 == 0 else self.ptkB
            vector.wait_ge(sem["s_pfx"], 22 + c)
            vector.tensor_copy(self.fv8[:, c * 8:(c + 1) * 8], bank[:, 0:8]).then_inc(sem["s_dfx"], 1)
        vector.wait_ge(sem["s_afx"], 2)
        vector.scalar_tensor_tensor(self.fk8[:, :], self.fkv8[:, :], 0.0,
                                    self.adjkc[:, :], ALU.max, ALU.add).then_inc(sem["s_dfx"], 1)
        vector.drain()
        vector.tensor_mul(self.fkv8[:, :], self.fk8[:, :], self.fv8[:, :]).then_inc(sem["s_dfx"], 1)
        vector.drain()
        for c in range(CH):
            vector.tensor_tensor_scan(
                self.n8kvc[:, c * 8:(c + 1) * 8], self.fkv8[:, c * 8:(c + 1) * 8],
                self.fkv8[:, c * 8:(c + 1) * 8], 0.0, ALU.add, ALU.bypass)
            ins = vector.tensor_tensor_scan(
                self.n8kc[:, c * 8:(c + 1) * 8], self.fk8[:, c * 8:(c + 1) * 8],
                self.fk8[:, c * 8:(c + 1) * 8], 0.0, ALU.add, ALU.bypass)
        ins.then_inc(sem["s_dfx"], 1)
        vector.drain()
        vector.tensor_copy(self.totsb[:, 0:16], self.ck[:, :])
        vector.tensor_copy(self.totsb[:, 16:32], self.ckv[:, :]).then_inc(sem["s_dfx"], 1)
        for c in range(CH):
            bank = (self.pmA, self.pmB, self.ptkA, self.ptkB)[c % 4]
            vector.wait_ge(sem["s_pfx"], 38 + c)
            vector.tensor_copy(self.fk8[:, c * 8:(c + 1) * 8], bank[:, 0:8]).then_inc(sem["s_dfx"], 1)
        vector.drain()
        vector.tensor_scalar_min(self.fv8[:, :], self.fk8[:, :], 0.0).then_inc(sem["s_dfx"], 1)
        vector.wait_ge(sem["s_afx"], 3)
        vector.scalar_tensor_tensor(self.fv8[:, :], self.fk8[:, :], 0.0,
                                    self.fkv8[:, :], ALU.max, ALU.add).then_inc(sem["s_dfx"], 1)
        vector.drain()
        vector.tensor_scalar(self.q8m[:, :], self.fv8[:, :], self.masks_sb[:, 0:1],
                             None, ALU.mult).then_inc(sem["s_dfx"], 1)
        vector.drain()
        vector.tensor_scalar(self.fx8[:, :], self.fo8[:, :], self.masks_sb[0:8, 2:3],
                             None, ALU.mult).then_inc(sem["s_fy"], 1)

    def fixup_rounds(self, vector):
        sem = self.sem
        acc = [self.accA, self.accB]
        vector.memset(self.accA[:, :], 0.0)
        vector.drain()
        for r in range(NCORES):
            vector.wait_ge(sem["s_fx8"], 16 * (self.fx8_gth0 + r))
            src, dst = acc[r % 2], acc[(r + 1) % 2]
            if r == 0:
                vector.tensor_copy(self.adjkc[:, 0:32], self.gth[:, 32:64])
            vector.scalar_tensor_tensor(dst[:, :], self.gth[:, :],
                                        self.pmsel_sb[:, r:r + 1], src[:, :],
                                        ALU.mult, ALU.add).then_inc(sem["s_dfx"], 2 if r == 0 else 1)
            vector.drain()
        fin = acc[NCORES % 2]
        vector.drain()
        vector.tensor_sub(self.dk[:, :], bass.AP(self.n8kc, 7, [[128, 128], [8, CH]]),
                          self.adjkc[:, 0:16])
        vector.tensor_sub(self.dkv[:, :], bass.AP(self.n8kvc, 7, [[128, 128], [8, CH]]),
                          self.adjkc[:, 16:32])
        vector.drain()
        vector.scalar_tensor_tensor(self.pk[:, :], self.dk[:, :], self.masks_sb[:, 1:2],
                                    fin[:, 0:16], ALU.mult, ALU.add)
        vector.scalar_tensor_tensor(self.pkv[:, :], self.dkv[:, :], self.masks_sb[:, 1:2],
                                    fin[:, 16:32], ALU.mult, ALU.add)
        vector.drain()
        for c in range(CH):
            vector.tensor_scalar(self.adjkvc[:, c * 8:(c + 1) * 8], self.n8kvc[:, c * 8:(c + 1) * 8],
                                 self.dkv[:, c:c + 1], self.masks_sb[:, 0:1], ALU.subtract, ALU.mult)
            ins = vector.tensor_scalar(self.adjkc[:, c * 8:(c + 1) * 8], self.n8kc[:, c * 8:(c + 1) * 8],
                                       self.dk[:, c:c + 1], self.masks_sb[:, 0:1], ALU.subtract, ALU.mult)
        ins.then_inc(sem["s_dfx"], 1)

    def rsqrt_chain8(self, vector):
        w = 1
        vector.tensor_scalar(self.st_i1[0:8, 0:w], self.st_v[0:8, 0:w].bitcast(I32), 9, None, ALU.logical_shift_right)
        vector.drain()
        vector.tensor_scalar(self.st_i2[0:8, 0:w], self.st_i1[0:8, 0:w], MAGIC8, -1, ALU.subtract, ALU.mult)
        vector.drain()
        vector.tensor_scalar(self.st_i1[0:8, 0:w], self.st_i2[0:8, 0:w], 8, None, ALU.logical_shift_left)
        vector.drain()
        cur = self.st_i1[0:8, 0:w].bitcast(F32)
        for it in range(4):
            nxt = self.st_rs[0:8, 0:w] if it == 3 else (self.st_r[0:8, 0:w] if it % 2 == 0 else self.st_r[0:8, 2:2 + w])
            vector.tensor_mul(self.st_t1[0:8, 0:w], cur, cur)
            vector.drain()
            vector.tensor_mul(self.st_t2[0:8, 0:w], self.st_t1[0:8, 0:w], self.st_v[0:8, 0:w])
            vector.drain()
            vector.tensor_scalar(self.st_t1[0:8, 0:w], self.st_t2[0:8, 0:w], -0.5, 1.5, ALU.mult, ALU.add)
            vector.drain()
            vector.tensor_mul(nxt, cur, self.st_t1[0:8, 0:w])
            vector.drain()
            cur = nxt



    def p3a_gpsimd(self, gpsimd):
        sem = self.sem
        gpsimd.wait_ge(sem["s_dfx"], 70)
        gpsimd.dma_start(out=self.wo_bf[:, :], in_=self.wot_in[:, :]).then_inc(sem["s_wco"], 16)
        gpsimd.dma_start(out=self.w1_bf[:, :], in_=self.w1t_in[:, :]).then_inc(sem["s_wc13"], 16)
        gpsimd.dma_start(out=self.w3_bf[:, :], in_=self.w3t_in[:, :]).then_inc(sem["s_wc13"], 16)

    def p3a_act(self, scalar):
        sem = self.sem
        nt = self.nt
        scalar.wait_ge(sem["s_pfx"], 4)
        scalar.wait_ge(sem["s_pkv"], 32 * nt)
        for j in range(min(2, nt)):
            scalar.wait_ge(sem["s_qw"] if False else sem["s_yw"], 0)
            self.dma(scalar, "s_yr", self.ycm3[:, (j % 2) * FT:(j % 2 + 1) * FT],
                     self.y_sp[j, :, :])
        scalar.wait_ge(sem["s_dfx"], 58)
        scalar.activation(self.fkv8[:, :], self.fv8[:, :], AF.Exp).then_inc(sem["s_afx"], 1)
        for i in range(nt):
            for c in range(CH):
                scalar.wait_ge(sem["s_dmin"], CH * nt + i * CH + c + 1)
                scalar.activation(self.texp3[:, (c % 4) * TILE:(c % 4 + 1) * TILE],
                                  self.tmin3[:, (c % 4) * TILE:(c % 4 + 1) * TILE],
                                  AF.Exp).then_inc(sem["s_aex"], 1)
            if i + 2 < nt:
                scalar.wait_ge(sem["s_pq"], 16 * (i + 1))
                self.dma(scalar, "s_yr", self.ycm3[:, ((i + 2) % 2) * FT:((i + 2) % 2 + 1) * FT],
                         self.y_sp[i + 2, :, :])
            scalar.wait_ge(sem["s_eq"], 16 * (i + 1))
            self.dma(scalar, "s_qw", self.q_sp[i, :, :], self.qst[:, :])

    def p3a_tensor(self, tensor):
        sem = self.sem
        nt = self.nt
        for c in range(CH):
            bank = (self.pmA, self.pmB, self.ptkA, self.ptkB)[c % 4]
            g = c // 4
            if c % 4 == 0:
                if g < 2:
                    tensor.wait_ge(sem["s_wkv2"], 16 * (5 + g))
                else:
                    tensor.wait_ge(sem["s_wt"], 16 * (self.wt_wq0 + g - 1))
            if c == 2 or c == 3:
                tensor.wait_ge(sem["s_dfx"], 34 + c)
            if c >= 4:
                tensor.wait_ge(sem["s_dfx"], 38 + c)
            for icc in range(CH):
                ins = tensor.matmul(
                    bank[:, 0:8],
                    self.wq_sb[:, icc * D + c * 128:icc * D + (c + 1) * 128],
                    self.y8c[:, icc * 8:(icc + 1) * 8],
                    start=(icc == 0), stop=(icc == CH - 1))
            ins.then_inc(sem["s_pfx"], 1)
        tensor.wait_ge(sem["s_wkv2"], 16 * 6)
        tensor.wait_ge(sem["s_wt"], 16 * (self.wt_wq0 + 2))
        for i in range(nt):
            tensor.wait_ge(sem["s_yr"], 16 * (nt + i + 1))
            for occ in range(CH):
                k = i * CH + occ
                bank = (self.pmA, self.pmB, self.ptkA, self.ptkB)[occ % 4]
                if k < 4:
                    tensor.wait_ge(sem["s_dfx"], 54 + k)
                else:
                    tensor.wait_ge(sem["s_eq"], k - 3)
                for icc in range(CH):
                    ins = tensor.matmul(
                        bank[:, 0:TILE],
                        self.wq_sb[:, icc * D + occ * 128:icc * D + (occ + 1) * 128],
                        self.ycm3[:, (i % 2) * FT + icc * TILE:(i % 2) * FT + (icc + 1) * TILE],
                        start=(icc == 0), stop=(icc == CH - 1))
                ins.then_inc(sem["s_pq"], 1)

    def p3a_dve(self, vector):
        sem = self.sem
        nt = self.nt
        def qmin(i, c):
            k = i * CH + c
            bank = (self.pmA, self.pmB, self.ptkA, self.ptkB)[c % 4]
            vector.wait_ge(sem["s_pq"], k + 1)
            if i == 0 and c == 0:
                vector.wait_ge(sem["s_dfx"], 40)
                vector.wait_ge(sem["s_gkv"], CH * nt + 1)
                vector.wait_ge(sem["s_fx8"], 16 * self.fx8_om)
            vector.tensor_scalar_min(
                self.tmin3[:, (c % 4) * TILE:(c % 4 + 1) * TILE],
                bank[:, 0:TILE], 0.0).then_inc(sem["s_dmin"], 1)

        def qstt(i, c):
            k = i * CH + c
            bank = (self.pmA, self.pmB, self.ptkA, self.ptkB)[c % 4]
            if c == 0 and i >= 1:
                vector.wait_ge(sem["s_qw"], 16 * i)
            vector.wait_ge(sem["s_aex"], CH * nt + k + 1)
            vector.scalar_tensor_tensor(
                self.qst[:, c * TILE:(c + 1) * TILE],
                bank[:, 0:TILE], 0.0,
                self.texp3[:, (c % 4) * TILE:(c % 4 + 1) * TILE],
                ALU.max, ALU.add).then_inc(sem["s_eq"], 1)

        for i in range(nt):
            for c in range(CH):
                qmin(i, c)
                if c >= 1:
                    qstt(i, c - 1)
            qstt(i, CH - 1)
            if i == min(2, nt - 1):
                self.fixup_rounds(vector)


    def p3b_sync(self, sync):
        sem = self.sem
        nt = self.nt
        H = FT // 2

        def kload(i):
            for h in range(2):
                if i >= 1:
                    sync.wait_ge(sem["s_dden"], 16 * (i - 1) + 8 * (h + 1))
                self.dma(sync, "s_kr", self.kvcb[:, h * H:(h + 1) * H],
                         self.kvc_sp[i, :, h * H:(h + 1) * H])
                self.dma(sync, "s_kr", self.kcb[:, h * H:(h + 1) * H],
                         self.kc_sp[i, :, h * H:(h + 1) * H])

        sync.wait_ge(sem["s_kw"], 32)
        sync.wait_ge(sem["s_pfx"], 37)
        kload(0)
        sync.wait_ge(sem["s_pq"], 16 * nt)
        sync.wait_ge(sem["s_wco"], 16)
        for g in range(4):
            self.dma(sync, "s_wkv2",
                     bass.AP(self.wo_sb, g * 512, [[self.wo_sb.shape[1], 128], [D, CH], [1, 512]]),
                     self.wo_bf[:, g * 512:(g + 1) * 512].rearrange("(k p) o -> p k o", p=128))
        sync.wait_ge(sem["s_kw"], 32 * nt)
        for i in range(1, nt):
            kload(i)

    def p3b_gpsimd(self, gpsimd):
        sem = self.sem
        nt = self.nt
        def patch_rows():
            gpsimd.wait_ge(sem["s_eao"], 4)
            c0 = self.dcnt["s_fx8"]
            gpsimd.wait_ge(sem["s_fx8"], 16 * c0)
            gpsimd.dma_start(out=self.x2b[0:8, 0:D], in_=self.out8m_sp[:, :],
                             accum_op=ALU.add).then_inc(sem["s_fx8"], 16)
            self.dcnt["s_fx8"] = c0 + 1
            self.fx8_pat = self.dcnt["s_fx8"]

        def xload(idx):
            if idx >= 3:
                gpsimd.wait_ge(sem["s_eao"], (idx - 2) * 4)
            self.dma(gpsimd, "s_x2",
                     self.xs3[:, (idx % 3) * D:(idx % 3 + 1) * D],
                     self.x_in[idx * 128:(idx + 1) * 128, :])

        gpsimd.wait_ge(sem["s_pq"], 16 * nt)
        for idx in range(min(3, S * nt)):
            xload(idx)
        for i in range(nt):
            for c in range(CH):
                gpsimd.wait_ge(sem["s_aln"], i * CH + c + 1)
                gpsimd.tensor_tensor(
                    self.ratio[:, (i % 2) * FT + c * TILE:(i % 2) * FT + (c + 1) * TILE],
                    self.numb[:, (c % 8) * TILE:(c % 8 + 1) * TILE],
                    self.recb[:, (c % 8) * TILE:(c % 8 + 1) * TILE],
                    ALU.mult).then_inc(sem["s_dr"], 1)
            if i == min(1, nt - 1):
                patch_rows()
            if i >= 1:
                for idx in (2 * i + 1, 2 * i + 2):
                    if idx < S * nt:
                        xload(idx)

    def p3b_act(self, scalar):
        sem = self.sem
        nt = self.nt
        scalar.wait_ge(sem["s_pfx"], 37)
        H = FT // 2

        def qload(j):
            scalar.wait_ge(sem["s_qw"], 16 * (j + 1))
            for h in range(2):
                if j >= 1:
                    scalar.wait_ge(sem["s_dden"], 16 * (j - 1) + 8 * (h + 1))
                self.dma(scalar, "s_qr", self.qb[:, h * H:(h + 1) * H],
                         self.q_sp[j, :, h * H:(h + 1) * H])

        def recip(j):
            for g in range(2):
                for c in range(g * 8, g * 8 + 8):
                    scalar.wait_ge(sem["s_dden"], j * CH + c + 1)
                    scalar.activation(self.lnb[:, (c % 8) * TILE:(c % 8 + 1) * TILE],
                                      self.denb[:, (c % 8) * TILE:(c % 8 + 1) * TILE],
                                      AF.Ln, bias=self.consts[:, 1:2])
                scalar.drain()
                for c in range(g * 8, g * 8 + 8):
                    scalar.activation(self.recb[:, (c % 8) * TILE:(c % 8 + 1) * TILE],
                                      self.lnb[:, (c % 8) * TILE:(c % 8 + 1) * TILE],
                                      AF.Exp, scale=self.consts[:, 2:3]).then_inc(sem["s_aln"], 1)
                scalar.drain()

        def spill_y3(j):
            scalar.wait_ge(sem["s_etp"], 2 * S * nt + (j + 1) * 2 * S)
            self.dma(scalar, "s_3w", self.y3_sp[j, :, :], self.y3cm[:, :])

        qload(0)
        recip(0)
        for i in range(nt):
            if i + 1 < nt:
                qload(i + 1)
                recip(i + 1)
            for s in range(S):
                idx = i * S + s
                scalar.wait_ge(sem["s_eao"], i * 8 + 4 * (s + 1))
                if i == 0:
                    scalar.wait_ge(sem["s_fx8"], 16 * self.fx8_pat)
                if idx:
                    scalar.wait_ge(sem["s_ast"], nt * S + idx)
                xa = self.x2b[:, s * D:(s + 1) * D]
                col = (i % 2) * S + s
                scalar.activation(self.scr[:, :], xa, AF.Copy, accum_out=self.st_sum[:, col:col + 1])
                scalar.drain()
                scalar.activation(self.scr[:, :], xa, AF.Square,
                                  accum_out=self.st_sq[:, col:col + 1]).then_inc(sem["s_ast"], 1)
            if i == 0:
                scalar.wait_ge(sem["s_eao"], 4)
                scalar.wait_ge(sem["s_fx8"], 16 * self.fx8_pat)
                self.dma(scalar, "s_ow", self.x28_sp[:, :], self.x2b[0:8, 0:D])
            scalar.wait_ge(sem["s_eao"], 8 * (i + 1))
            self.dma(scalar, "s_ow",
                     self.out[i * TILE:(i + 1) * TILE, :].rearrange("(s p) d -> p s d", p=128),
                     self.x2b[:, :].rearrange("p (s d) -> p s d", s=S))
            if i >= 1:
                spill_y3(i - 1)
        spill_y3(nt - 1)

    def p3b_tensor(self, tensor):
        sem = self.sem
        nt = self.nt

        def tp(j):
            for s in range(S):
                tensor.wait_ge(sem["s_dy"], nt * S + j * S + s + 1)
                for g in range(2):
                    kk = 2 * S * nt + (j * S + s) * 2 + g
                    if kk >= 2 * S * nt + 2:
                        tensor.wait_ge(sem["s_etp"], kk - 1)
                    for jj in range(8):
                        cj = g * 8 + jj
                        ins = tensor.transpose(
                            self.ptr[:, cj * 128:(cj + 1) * 128],
                            self.ytk3[:, s * D + cj * 128:s * D + (cj + 1) * 128],
                            self.ident_bf[:, :])
                    ins.then_inc(sem["s_ptp"], 1)

        for i in range(nt):
            tensor.wait_ge(sem["s_dr"], 16 * (i + 1))
            for s in range(S):
                for ob in range(4):
                    k = i * 8 + s * 4 + ob
                    bank = (self.ptkA, self.ptkB, self.pmA, self.pmB)[k % 4]
                    if i == 0:
                        tensor.wait_ge(sem["s_wkv2"], 16 * (7 + ob))
                    if k >= 4:
                        tensor.wait_ge(sem["s_eao"], k - 3)
                    for icc in range(CH):
                        ins = tensor.matmul(
                            bank[:, 0:512],
                            self.ratio[:, (i % 2) * FT + icc * TILE + s * 128:
                                       (i % 2) * FT + icc * TILE + s * 128 + 128],
                            self.wo_sb[:, icc * D + ob * 512:icc * D + (ob + 1) * 512],
                            start=(icc == 0), stop=(icc == CH - 1))
                    ins.then_inc(sem["s_pao"], 1)
            if i >= 1:
                tp(i - 1)
        tp(nt - 1)

    def p3b_dve(self, vector):
        sem = self.sem
        nt = self.nt

        def dens_g(j, g):
            vector.wait_ge(sem["s_qr"], 16 * (2 * j + g + 1))
            vector.wait_ge(sem["s_kr"], 16 * (4 * j + 2 * (g + 1)))
            if g == 1:
                vector.wait_ge(sem["s_dr"], j * CH + 8)
            elif j >= 1:
                vector.wait_ge(sem["s_dr"], j * CH)
            vector.drain()
            for c in range(g * 8, g * 8 + 8):
                vector.scalar_tensor_tensor(
                    self.denb[:, (c % 8) * TILE:(c % 8 + 1) * TILE],
                    self.kcb[:, c * TILE:(c + 1) * TILE], self.pk[:, c:c + 1],
                    self.qb[:, c * TILE:(c + 1) * TILE], ALU.add, ALU.mult)
                vector.scalar_tensor_tensor(
                    self.numb[:, (c % 8) * TILE:(c % 8 + 1) * TILE],
                    self.kvcb[:, c * TILE:(c + 1) * TILE], self.pkv[:, c:c + 1],
                    self.qb[:, c * TILE:(c + 1) * TILE], ALU.add, ALU.mult).then_inc(sem["s_dden"], 1)

        def evacs(j):
            if j >= 1:
                vector.wait_ge(sem["s_3w"], 16 * j)
            for s in range(S):
                for g in range(2):
                    kk = 2 * S * nt + (j * S + s) * 2 + g
                    vector.wait_ge(sem["s_ptp"], kk + 1)
                    for jj in range(8):
                        cj = g * 8 + jj
                        ins = vector.tensor_copy(
                            self.y3cm[:, cj * TILE + s * 128:cj * TILE + s * 128 + 128],
                            self.ptr[:, cj * 128:(cj + 1) * 128])
                    ins.then_inc(sem["s_etp"], 1)

        for i in range(nt):
            if i == 0:
                vector.wait_ge(sem["s_qr"], 32)
                vector.wait_ge(sem["s_kr"], 64)
                vector.wait_ge(sem["s_x2"], 0)
                vector.wait_ge(sem["s_dfx"], 70)
                tmp = bass.AP(self.fk8, 0, [[128, 128], [8, CH], [1, 8]])
                for (buf, adj) in ((self.kvcb, self.adjkvc), (self.kcb, self.adjkc),
                                   (self.qb, self.q8m)):
                    bv = bass.AP(buf, 0, [[buf.shape[1], 128], [TILE, CH], [1, 8]])
                    av = bass.AP(adj, 0, [[128, 128], [8, CH], [1, 8]])
                    vector.scalar_tensor_tensor(tmp, bv, self.masks_sb[:, 3:4], av,
                                                ALU.mult, ALU.add)
                    vector.drain()
                    ins = vector.tensor_copy(bv, tmp)
                    vector.drain()
                ins.then_inc(sem["s_fy"], 1)
                dens_g(0, 0)
                dens_g(0, 1)
            nxt = i + 1 < nt
            if nxt:
                if i >= 1:
                    vector.wait_ge(sem["s_pao"], 8 * i)
                dens_g(i + 1, 0)
            if i == 0:
                vector.wait_ge(sem["s_qw"], 16 * nt)
            if i >= 1:
                vector.wait_ge(sem["s_ow"], 16 * (i + 1))
                vector.drain()
            for s in range(S):
                for ob in range(4):
                    k = i * 8 + s * 4 + ob
                    bank = (self.ptkA, self.ptkB, self.pmA, self.pmB)[k % 4]
                    vector.wait_ge(sem["s_pao"], k + 1)
                    idx = i * S + s
                    vector.wait_ge(sem["s_x2"], 16 * (idx + 1))
                    vector.scalar_tensor_tensor(
                        self.x2b[:, s * D + ob * 512:s * D + (ob + 1) * 512],
                        bank[:, 0:512], ATT_SCALE,
                        self.xs3[:, (idx % 3) * D + ob * 512:(idx % 3) * D + (ob + 1) * 512],
                        ALU.mult, ALU.add).then_inc(sem["s_eao"], 1)
                if s == 0 and nxt:
                    dens_g(i + 1, 1)
            if i >= 1:
                evacs(i - 1)
            vector.wait_ge(sem["s_ast"], nt * S + i * S + S)
            xs_ = [self.x2b[:, s * D:(s + 1) * D] for s in range(S)]
            yd = [self.ytk3[:, s * D:(s + 1) * D] for s in range(S)]
            self.ln_finalize_apply(vector, xs_, yd, "s_dy", None, b=(i % 2) * S)
        evacs(nt - 1)


    def p4_gpsimd(self, gpsimd):
        sem = self.sem
        nt = self.nt
        gpsimd.wait_ge(sem["s_pao"], 8 * nt)
        gpsimd.wait_ge(sem["s_dden"], 16 * nt)
        gpsimd.wait_ge(sem["s_dr"], 16 * nt)
        gpsimd.wait_ge(sem["s_eao"], 8 * nt)
        gpsimd.wait_ge(sem["s_wc13"], 32)
        self.wt_w140 = self.dcnt["s_wt"]
        for g in range(4):
            c = self.dcnt["s_wt"]
            gpsimd.wait_ge(sem["s_wt"], 16 * c)
            self.wload_ocgroup(gpsimd, self.w1_sb, self.w1_bf, g)
            self.dcnt["s_wt"] = c + 1
        for g in range(4):
            c = self.dcnt["s_wt"]
            gpsimd.wait_ge(sem["s_wt"], 16 * c)
            self.wload_ocgroup(gpsimd, self.w3_sb, self.w3_bf, g)
            self.dcnt["s_wt"] = c + 1
        gpsimd.wait_ge(sem["s_ps2"], 4)
        self.dma(gpsimd, "s_fx8", self.s2b_ci[:, :], self.s2sb[:, :])
        gpsimd.wait_ge(sem["s_fx8"], 16 * self.dcnt["s_fx8"])
        if self.use_cc:
            gpsimd.collective_compute(
                "AllReduce", ALU.add, replica_groups=[list(range(NCORES))],
                ins=[self.s2b_ci.ap()], outs=[self.s2b_co.ap()],
            ).then_inc(sem["s_cc"], 1)
        else:
            self.dma(gpsimd, "s_fx8", self.s2b_co[:, :], self.s2b_ci[:, :])
            gpsimd.wait_ge(sem["s_fx8"], 16 * self.dcnt["s_fx8"])
            gpsimd.nop().then_inc(sem["s_cc"], 1)
        self.p5_gpsimd(gpsimd)

    def p4_act(self, scalar):
        sem = self.sem
        nt = self.nt
        b_lg = S * nt
        b_h1 = CH * nt
        b_h2 = 8 * nt
        b_es2 = S * nt - 1
        scalar.wait_ge(sem["s_eao"], 8 * nt)
        scalar.wait_ge(sem["s_ow"], 16 * self.dcnt["s_ow"])
        for j in range(min(2, nt)):
            scalar.wait_ge(sem["s_3w"], 16 * (j + 1))
            self.dma(scalar, "s_3r", self.ycm4[:, (j % 2) * FT:(j % 2 + 1) * FT],
                     self.y3_sp[j, :, :])
        for i in range(nt):
            for s in range(S):
                idx = i * S + s
                scalar.wait_ge(sem["s_dd"], b_lg + idx + 1)
                scalar.activation(self.rsl[:, s:s + 1], self.rd[:, s:s + 1], AF.Silu).then_inc(sem["s_asl"], 1)
            for occ in range(CH):
                k = i * CH + occ
                scalar.wait_ge(sem["s_ph1"], b_h1 + k + 1)
                bank = self.pmA if occ % 2 == 0 else self.pmB
                scalar.activation(self.h14[:, occ * TILE:(occ + 1) * TILE],
                                  bank[:, 0:TILE], AF.Silu).then_inc(sem["s_eh1"], 1)
            for s in range(S):
                idx = i * S + s
                if idx >= 2:
                    scalar.wait_ge(sem["s_es2"], b_es2 + idx - 1)
                for ob in range(4):
                    k = idx * 4 + ob
                    scalar.wait_ge(sem["s_ph2"], b_h2 + k + 1)
                    bank = self.ptkA if k % 2 == 0 else self.ptkB
                    scalar.activation(
                        self.h2t4[:, (idx % 2) * D + ob * 512:(idx % 2) * D + (ob + 1) * 512],
                        bank[:, 0:512], AF.Silu).then_inc(sem["s_eh2"], 1)
            if i + 2 < nt:
                scalar.wait_ge(sem["s_3w"], 16 * (i + 3))
                scalar.wait_ge(sem["s_ph1"], b_h1 + (i + 1) * CH)
                self.dma(scalar, "s_3r", self.ycm4[:, ((i + 2) % 2) * FT:((i + 2) % 2 + 1) * FT],
                         self.y3_sp[i + 2, :, :])
        self.p5_act(scalar)

    def p4_dve(self, vector):
        sem = self.sem
        nt = self.nt
        b_lg = S * nt
        for i in range(nt):
            for s in range(S):
                self.router_dve_p4(vector, i, s, b_lg)
        vector.wait_ge(sem["s_ps2"], 3)
        vector.tensor_copy(self.s2sb[:, :], self.ps2[:, 0:128]).then_inc(sem["s_ps2"], 1)
        self.p5_dve(vector)

    def router_dve_p4(self, vector, i, s, base):
        sem = self.sem
        idx = i * S + s
        L = self.lg[:, s * 8:(s + 1) * 8]
        T1 = self.rt1[:, s * 8:(s + 1) * 8]
        T2 = self.rt2[:, s * 8:(s + 1) * 8]
        vector.wait_ge(sem["s_plg"], base + idx + 1)
        vector.tensor_copy(L, self.pmisc[:, s * 8:(s + 1) * 8]).then_inc(sem["s_elg"], 1)
        vector.drain()
        vector.reduce_max(self.rm1[:, s:s + 1], L, axis=AX.X)
        vector.drain()
        vector.tensor_scalar(T1, L, self.rm1[:, s:s + 1], None, ALU.is_ge)
        vector.drain()
        vector.scalar_tensor_tensor(T2, T1, -1e30, L, ALU.mult, ALU.add)
        vector.drain()
        vector.reduce_max(self.rm2[:, s:s + 1], T2, axis=AX.X)
        vector.drain()
        vector.tensor_sub(self.rd[:, s:s + 1], self.rm1[:, s:s + 1], self.rm2[:, s:s + 1])
        vector.drain()
        vector.tensor_scalar(self.rd[:, s:s + 1], self.rd[:, s:s + 1], 1e-30, None, ALU.add).then_inc(sem["s_dd"], 1)
        vector.tensor_scalar(T2, T2, self.rm2[:, s:s + 1], None, ALU.is_ge)
        vector.wait_ge(sem["s_asl"], base + idx + 1)
        vector.reciprocal(self.rg1[:, s:s + 1], self.rd[:, s:s + 1])
        vector.drain()
        vector.tensor_mul(self.rg1[:, s:s + 1], self.rg1[:, s:s + 1], self.rsl[:, s:s + 1])
        vector.drain()
        vector.tensor_scalar(self.rg2[:, s:s + 1], self.rg1[:, s:s + 1], -1.0, 1.0, ALU.mult, ALU.add)
        vector.drain()
        vector.tensor_scalar(T1, T1, self.rg1[:, s:s + 1], None, ALU.mult)
        vector.drain()
        vector.scalar_tensor_tensor(self.G[:, s * 8:(s + 1) * 8], T2, self.rg2[:, s:s + 1],
                                    T1, ALU.mult, ALU.add).then_inc(sem["s_dg"], 1)

    def p4_tensor(self, tensor):
        sem = self.sem
        nt = self.nt
        b_lg = S * nt
        b_h1 = CH * nt
        b_h2 = 8 * nt
        b_es2 = S * nt - 1
        b_etp = 4 * S * nt
        tensor.wait_ge(sem["s_eao"], 8 * nt)
        for i in range(nt):
            tensor.wait_ge(sem["s_3r"], 16 * (i + 1))
            for s in range(S):
                idx = i * S + s
                if b_lg + idx >= 1:
                    tensor.wait_ge(sem["s_elg"], b_lg + idx)
                for icc in range(CH):
                    ins = tensor.matmul(
                        self.pmisc[:, s * 8:(s + 1) * 8],
                        self.ycm4[:, (i % 2) * FT + icc * TILE + s * 128:
                                 (i % 2) * FT + icc * TILE + s * 128 + 128],
                        self.wrt_sb[:, icc * 8:(icc + 1) * 8],
                        start=(icc == 0), stop=(icc == CH - 1))
                ins.then_inc(sem["s_plg"], 1)
            for occ in range(CH):
                k = i * CH + occ
                bank = self.pmA if occ % 2 == 0 else self.pmB
                if b_h1 + k >= 2:
                    tensor.wait_ge(sem["s_eh1"], b_h1 + k - 1)
                if i == 0:
                    tensor.wait_ge(sem["s_wt"], 16 * (self.wt_w140 + occ // 4 + 1))
                for icc in range(CH):
                    ins = tensor.matmul(
                        bank[:, 0:TILE],
                        self.w1_sb[:, icc * D + occ * 128:icc * D + (occ + 1) * 128],
                        self.ycm4[:, (i % 2) * FT + icc * TILE:(i % 2) * FT + (icc + 1) * TILE],
                        start=(icc == 0), stop=(icc == CH - 1))
                ins.then_inc(sem["s_ph1"], 1)
            for s in range(S):
                idx = i * S + s
                tensor.wait_ge(sem["s_eh1"], b_h1 + (i + 1) * CH)
                for ob in range(4):
                    k = idx * 4 + ob
                    bank = self.ptkA if k % 2 == 0 else self.ptkB
                    if b_h2 + k >= 2:
                        tensor.wait_ge(sem["s_eh2"], b_h2 + k - 1)
                    if i == 0:
                        tensor.wait_ge(sem["s_wt"], 16 * (self.wt_w140 + 4 + ob + 1))
                    for icc in range(CH):
                        ins = tensor.matmul(
                            bank[:, 0:512],
                            self.h14[:, icc * TILE + s * 128:icc * TILE + s * 128 + 128],
                            self.w3_sb[:, icc * D + ob * 512:icc * D + (ob + 1) * 512],
                            start=(icc == 0), stop=(icc == CH - 1))
                    ins.then_inc(sem["s_ph2"], 1)
                tensor.wait_ge(sem["s_eh2"], b_h2 + (idx + 1) * 4)
                tensor.wait_ge(sem["s_dg"], b_lg + idx + 1)
                first = (i == 0 and s == 0)
                last = (i == nt - 1 and s == S - 1)
                for c in range(CH):
                    ins = tensor.matmul(
                        self.ps2[:, c * 8:(c + 1) * 8],
                        self.h2t4[:, (idx % 2) * D + c * 128:(idx % 2) * D + (c + 1) * 128],
                        self.G[:, s * 8:(s + 1) * 8],
                        start=(first and c == 0), stop=(last and c == CH - 1))
                if last:
                    ins.then_inc(sem["s_ps2"], 1)
                else:
                    ins.then_inc(sem["s_es2"], 1)
        self.p5_tensor(tensor)


    def p5_gpsimd(self, gpsimd):
        sem = self.sem
        gpsimd.wait_ge(sem["s_cc"], 3)
        self.dma(gpsimd, "s_fx8", self.fs2b[:, :], self.s2b_co[:, :])
        gpsimd.wait_ge(sem["s_fx8"], 16 * self.dcnt["s_fx8"])
        gpsimd.tensor_copy(self.fs2c[:, :], self.fs2b[:, :]).then_inc(sem["s_gkv"], 2)
        self.dma(gpsimd, "s_fx8", self.fx28[:, :], self.x28_sp[:, :])
        self.fx8_x28 = self.dcnt["s_fx8"]
        self.wt_w250 = self.dcnt["s_wt"] + 1
        for b in range(4):
            if b >= 1:
                gpsimd.wait_ge(sem["s_pfx"], 53 + b)
            self.dma(gpsimd, "s_wt",
                     self.w2s5[:, :].rearrange("p (k d) -> p k d", k=4),
                     self.w2_bf[b * 512:(b + 1) * 512, :].rearrange("(k p) d -> p k d", p=128))

    def p5_tensor(self, tensor):
        sem = self.sem
        banks = [self.pmA, self.pmB, self.ptkA, self.ptkB]
        tensor.wait_ge(sem["s_gkv"], CH * self.nt + 3)
        for b in range(4):
            tensor.wait_ge(sem["s_wt"], 16 * (self.wt_w250 + b))
            for il in range(4):
                icc = b * 4 + il
                for ob in range(4):
                    ins = tensor.matmul(
                        banks[ob][0:8, 0:512],
                        self.fs2c[:, icc * 8:(icc + 1) * 8],
                        self.w2s5[:, il * D + ob * 512:il * D + (ob + 1) * 512],
                        start=(icc == 0), stop=(icc == CH - 1))
            ins.then_inc(sem["s_pfx"], 1)

    def p5_dve(self, vector):
        sem = self.sem
        banks = [self.pmA, self.pmB, self.ptkA, self.ptkB]
        vector.wait_ge(sem["s_pfx"], 57)
        for ob in range(4):
            ins = vector.tensor_copy(self.fo8b[:, ob * 512:(ob + 1) * 512], banks[ob][0:8, 0:512])
        ins.then_inc(sem["s_dfx"], 1)
        vector.wait_ge(sem["s_fx8"], 16 * self.fx8_x28)
        vector.drain()
        vector.scalar_tensor_tensor(self.fo8b[:, :], self.fo8b[:, :],
                                    self.masks_sb[0:8, 2:3], self.fx28[:, :],
                                    ALU.mult, ALU.add).then_inc(sem["s_dfx"], 1)

    def p5_act(self, scalar):
        sem = self.sem
        scalar.wait_ge(sem["s_dfx"], 72)
        self.dma(scalar, "s_ow", self.out[0:8, :], self.fo8b[:, :])
        scalar.wait_ge(sem["s_ow"], 16 * self.dcnt["s_ow"])
        scalar.wait_ge(sem["s_3w"], 16 * self.nt)


    def build(self):
        from contextlib import ExitStack
        nc = self.nc
        with ExitStack() as ctx:
            self.declare()
            self.alloc_sbuf()
            self.semaphores(ctx)
            block = ctx.enter_context(nc.Block())
            mk = self
            sem = self.sem

            @block.sync
            def _(sync):
                sync.wait_ge(sem["s_init"], 1)
                mk.p1_sync(sync)
                mk.rest_sync(sync)

            @block.gpsimd
            def _(gpsimd):
                mk.p1_gpsimd(gpsimd)
                mk.rest_gpsimd(gpsimd)

            @block.scalar
            def _(scalar):
                scalar.wait_ge(sem["s_wt"], 16 * 4)
                scalar.wait_ge(sem["s_init"], 1)
                mk.p1_act(scalar)
                mk.rest_act(scalar)

            @block.vector
            def _(vector):
                vector.wait_ge(sem["s_init"], 1)
                mk.p1_dve(vector)
                mk.rest_dve(vector)

            @block.tensor
            def _(tensor):
                tensor.wait_ge(sem["s_wt"], 16 * 4)
                tensor.wait_ge(sem["s_init"], 1)
                mk.p1_tensor(tensor)
                mk.rest_tensor(tensor)

        return nc


    def rest_sync(self, sync):
        if self.stop_after >= 2:
            self.p2_sync(sync)
        if self.stop_after >= 3:
            self.p3b_sync(sync)

    def rest_gpsimd(self, gpsimd):
        sem = self.sem
        if self.stop_after >= 2:
            self.p2_gpsimd(gpsimd)
        if self.stop_after >= 3:
            self.p3a_gpsimd(gpsimd)
            self.p3b_gpsimd(gpsimd)
        if self.stop_after >= 4:
            self.p4_gpsimd(gpsimd)
        if self.dbg:
            gpsimd.wait_ge(sem["s_cc"], 1)
            self.dma(gpsimd, "s_fx8", self.s2_dbg[:, :], self.s2_co[:, :])
            gpsimd.wait_ge(sem["s_fx8"], 16 * self.dcnt["s_fx8"])

    def rest_act(self, scalar):
        if self.stop_after >= 2:
            self.p2_act(scalar)
        if self.stop_after >= 3:
            self.p3a_act(scalar)
            self.p3b_act(scalar)
        if self.stop_after >= 4:
            self.p4_act(scalar)
        if self.stop_after == 2:
            scalar.wait_ge(self.sem["s_kw"], 32 * self.nt)
            scalar.wait_ge(self.sem["s_dfx"], 43 + NCORES)
        if self.stop_after == 3:
            scalar.wait_ge(self.sem["s_ow"], 16 * self.dcnt["s_ow"])
            scalar.wait_ge(self.sem["s_3w"], 16 * self.nt)
        scalar.wait_ge(self.sem["s_yw"], 16 * self.nt)

    def rest_dve(self, vector):
        if self.stop_after >= 2:
            self.p2_dve(vector)
        if self.stop_after >= 3:
            self.p3a_dve(vector)
            self.p3b_dve(vector)
        if self.stop_after >= 4:
            self.p4_dve(vector)

    def rest_tensor(self, tensor):
        if self.stop_after >= 2:
            self.p2_tensor(tensor)
        if self.stop_after >= 3:
            self.p3a_tensor(tensor)
            self.p3b_tensor(tensor)
        if self.stop_after >= 4:
            self.p4_tensor(tensor)



def make_core_inputs(x_shard, x8, wq, wkv, wo, w1, w3, w2, wr, core):
    masks = np.zeros((128, 4), dtype=np.float32)
    if core == 0:
        masks[:, 0] = 1.0
        masks[:, 2] = 0.5
    masks[:, 3] = 1.0 - masks[:, 0]
    if core in (0, 1):
        masks[:, 1] = 1.0
    pmsel = np.zeros((128, 8), dtype=np.float32)
    if core % 2 == 1:
        pmsel[:, core - 1] = 1.0
    return {
        "x": x_shard, "x8": x8,
        "wqt": wq, "wkvt": wkv, "wot": wo,
        "w1t": w1, "w3t": w3, "w2t": w2, "wrt": wr,
        "ident": np.eye(128, dtype=np.float32),
        "masks": masks, "pmsel": pmsel,
    }


def prep_weights(W_qkv, W_out, w1, w3, w2, wr):
    wq = np.ascontiguousarray(np.asarray(W_qkv)[0:D].T.astype(np.float32))
    wkv = np.ascontiguousarray(np.asarray(W_qkv)[D:3 * D].T.astype(np.float32))
    wo = np.ascontiguousarray(np.asarray(W_out).T.astype(np.float32))
    w1t = np.ascontiguousarray(np.asarray(w1).T.astype(np.float32))
    w3t = np.ascontiguousarray(np.asarray(w3).T.astype(np.float32))
    w2t = np.ascontiguousarray(np.asarray(w2).T.astype(np.float32))
    wrt = np.ascontiguousarray(np.asarray(wr).T.astype(np.float32)
                               .reshape(16, 128, 8).transpose(1, 0, 2).reshape(128, 128))
    return wq, wkv, wo, w1t, w3t, w2t, wrt


def kernel(L0_attn_x, L0_attn_W_qkv, L0_attn_W_out, L0_ffn_w1, L0_ffn_w3,
           L0_ffn_w2, L0_ffn_wr):
    x = np.ascontiguousarray(np.asarray(L0_attn_x, dtype=np.float32)).reshape(-1, D)
    wq, wkv, wo, w1t, w3t, w2t, wrt = prep_weights(
        L0_attn_W_qkv, L0_attn_W_out, L0_ffn_w1, L0_ffn_w3, L0_ffn_w2, L0_ffn_wr)
    x8 = np.ascontiguousarray(x[0:8])
    shards = [np.ascontiguousarray(x[c * TPC:(c + 1) * TPC]) for c in range(NCORES)]
    mk = MK()
    nc = mk.build()
    in_maps = [
        make_core_inputs(shards[c], x8, wq, wkv, wo, w1t, w3t, w2t, wrt, c)
        for c in range(NCORES)
    ]
    res = run_bass_kernel_spmd(nc, in_maps, core_ids=list(range(NCORES)))
    out = np.concatenate([r["out"] for r in res.results], axis=0)
    return out.reshape(4, 8192, D)
